# version 04
# baseline (speedup 1.0000x reference)
import numpy as np
import ml_dtypes
import concourse.bass as bass
import concourse.mybir as mybir
from concourse.bass_utils import run_bass_kernel_spmd
from contextlib import ExitStack

F32 = mybir.dt.float32
BF16 = mybir.dt.bfloat16
AF = mybir.ActivationFunctionType
ALU = mybir.AluOpType
AX = mybir.AxisListType

P = 128
D = 1024
NH = 16
DH = 64
I_FFN = 4096
NE = 4096
NK = 64
TOK = 4096
G = 512
NG = TOK // G
TPG = G // P
NT = TOK // P
ENGINES = ("sync", "tensor", "vector", "scalar", "gpsimd")


class EngProxy:

    def __init__(self, b, eng):
        self._b = b
        self._eng = eng

    def __getattr__(self, meth):
        b, eng = self._b, self._eng

        def call(*a, inc=None, incn=1, **kw):
            if inc is None:
                b.q[eng].append(lambda e, m=meth, a=a, kw=kw: getattr(e, m)(*a, **kw))
                return None
            h = b.semh(inc)
            b.cnt[inc] += incn
            b.q[eng].append(lambda e, m=meth, a=a, kw=kw, h=h, n=incn: getattr(e, m)(*a, **kw).then_inc(h, n))
            return b.cnt[inc]

        return call


class Builder:
    def __init__(self, nc, es):
        self.nc = nc
        self.es = es
        self.q = {e: [] for e in ENGINES}
        self.sems = {}
        self.cnt = {}
        self.e = {e: EngProxy(self, e) for e in ENGINES}

    def semh(self, name):
        if name not in self.sems:
            self.sems[name] = self.es.enter_context(self.nc.semaphore(name))
            self.cnt[name] = 0
        return self.sems[name]

    def wait(self, eng, name, val=None):
        h = self.semh(name)
        v = self.cnt[name] if val is None else val
        self.q[eng].append(lambda e, h=h, v=v: e.wait_ge(h, v))

    def dma(self, eng, out, in_, sem_name="dma", **kw):
        h = self.semh(sem_name)
        self.cnt[sem_name] += 16
        self.q[eng].append(lambda e, o=out, i=in_, h=h, kw=dict(kw): e.dma_start(out=o, in_=i, **kw).then_inc(h, 16))
        return self.cnt[sem_name]

    def sem_inc(self, eng, name, n=1):
        h = self.semh(name)
        self.cnt[name] += n
        self.q[eng].append(lambda e, h=h, n=n: e.sem_inc(h, n))
        return self.cnt[name]

    def barrier(self, tag):
        name = "bar"
        h = self.semh(name)
        self.cnt[name] += len(ENGINES)
        v = self.cnt[name]
        for eng in ENGINES:
            self.q[eng].append(lambda e, h=h: e.sem_inc(h, 1))
        for eng in ENGINES:
            self.q[eng].append(lambda e, h=h, v=v: e.wait_ge(h, v))

    def run(self, block):
        qs = self.q

        @block.sync
        def _(e):
            for fn in qs["sync"]:
                fn(e)

        @block.tensor
        def _(e):
            for fn in qs["tensor"]:
                fn(e)

        @block.vector
        def _(e):
            for fn in qs["vector"]:
                fn(e)

        @block.scalar
        def _(e):
            for fn in qs["scalar"]:
                fn(e)

        @block.gpsimd
        def _(e):
            for fn in qs["gpsimd"]:
                fn(e)


def build_nc(debug=False, max_pass=99, p1_stage=99):
    nc = bass.Bass("TRN2", target_bir_lowering=False, detect_race_conditions=False)
    dram = {}

    def din(name, shape, dtype=F32):
        dram[name] = nc.dram_tensor(name, shape, dtype, kind="ExternalInput").ap()
        return dram[name]

    x_own = din("x_own", [TOK, D])
    x_pre = din("x_pre", [TOK, D])
    pre_scale = din("pre_scale", [P, 1])
    wqT = din("wqT", [D, D]); wkT = din("wkT", [D, D]); wvT = din("wvT", [D, D]); woT = din("woT", [D, D])
    bq_col = din("bq_col", [64, 16]); bk_col = din("bk_col", [64, 16])
    bkB = din("bkB", [P, D]); bvB = din("bvB", [P, D]); woB = din("woB", [P, D])
    WupT = din("WupT", [D, I_FFN]); WdownT = din("WdownT", [I_FFN, D])
    WqT_pk = din("WqT_pk", [D, 512])
    keysT = din("keysT", [64, 512])
    deT = din("deT", [D, NE]); ue_n = din("ue_n", [NE, D])
    ident_bf = din("ident_bf", [P, P], BF16)
    triu_f = din("triu_f", [P, P])

    out_d = nc.dram_tensor("out", [TOK, D], F32, kind="ExternalOutput").ap()

    skind = "ExternalOutput" if debug else "Internal"

    def scrt(name, shape, dtype):
        return nc.dram_tensor(name, shape, dtype, kind=skind).ap()

    scr_xnT = scrt("scr_xnT", [NG, D, G], BF16)
    scr_QT = scrt("scr_QT", [NG, 64, NH * G], BF16)
    scr_KT = scrt("scr_KT", [NG, 64, NH * G], BF16)
    scr_Kn = scrt("scr_Kn", [NG, G, D], BF16)
    scr_V = scrt("scr_V", [NG, G, D], BF16)
    scr_EEx = scrt("scr_EEx", [NT, P, 4 * NK], BF16)
    scr_EEy = scrt("scr_EEy", [NT, P, 4 * NK], BF16)
    scr_acc = scrt("scr_acc", [TOK, D], F32)
    if debug:
        scr_T = scrt("scr_T", [P, NT * 4], F32)
        scr_rZ = scrt("scr_rZ", [P, NT * 4], F32)
        scr_S = scrt("scr_S", [64, 1024], F32)

    es = ExitStack()
    with es:
        b = Builder(nc, es)
        SY, TE, VE, SC, GP = (b.e[k] for k in ENGINES)

        def sb(name, shape, dtype):
            return es.enter_context(nc.sbuf_tensor(name, shape, dtype))

        c_ident = sb("c_ident", [P, P], BF16)
        c_triu = sb("c_triu", [P, P], F32)
        c_presc = sb("c_presc", [P, 1], F32)
        s_T = sb("s_T", [P, NT * 4], F32)
        s_rZ = sb("s_rZ", [P, NT * 4], F32)
        s_S = sb("s_S", [64, 1024], F32)
        s_Sbf = sb("s_Sbf", [64, 1024], BF16)
        s_woB = sb("s_woB", [P, D], F32)

        pbk = [es.enter_context(nc.psum_tensor(f"pb{i}", [P, 512], F32)) for i in range(7)]
        pbt = es.enter_context(nc.psum_tensor("pbt", [P, 1024], BF16))
        s_wu8 = sb("s_wu8", [P, 8], F32)
        s_wu64 = sb("s_wu64", [P, 64], F32)

        block = es.enter_context(nc.Block())

        b.dma("sync", c_ident[:], ident_bf, "ld")
        b.dma("sync", c_triu[:], triu_f, "ld")
        b.dma("sync", c_presc[:], pre_scale, "ld")
        b.dma("sync", s_woB[:], woB, "ld")
        b.wait("vector", "ld")
        b.wait("scalar", "ld")
        for rnd in range(2):
            VE.tensor_scalar_add(s_wu64[:], c_triu[:, 0:64], 1.0)
            VE.reduce_sum(s_wu8[:, 0:1], s_wu64[:], axis=AX.X)
            VE.scalar_tensor_tensor(s_wu64[:], s_wu64[:], 1.0, s_wu64[:],
                                    op0=ALU.mult, op1=ALU.mult, accum_out=s_wu8[:, 1:2])
            VE.max(out=s_wu8[:], in_=s_wu64[:])
            VE.match_replace(out=s_wu64[:], in_to_replace=s_wu8[:], in_values=s_wu64[:], imm_value=7.0)
            VE.reciprocal(s_wu8[:], s_wu8[:])
            VE.tensor_scalar(s_wu64[:], s_wu64[:], s_wu8[:, 0:1], s_wu8[:, 1:2],
                             op0=ALU.mult, op1=ALU.add, inc="wuv")
            b.wait("scalar", "wuv")
            SC.activation(s_wu64[:], s_wu64[:], func=AF.Square, scale=0.01,
                          accum_out=s_wu8[:, 2:3])
            SC.activation(s_wu64[:], s_wu64[:], func=AF.Ln, scale=0.001)
            SC.activation(s_wu64[:], s_wu64[:], func=AF.Exp, scale=0.001)
            SC.activation(s_wu64[:], s_wu64[:], func=AF.Silu, scale=0.1, inc="wus")
            b.wait("vector", "wus")
        b.barrier("warm")

        with ExitStack() as ps:
            def psb(name, shape, dtype):
                return ps.enter_context(nc.sbuf_tensor(name, shape, dtype))

            s_wkT = psb("s_wkT", [P, 8 * D], BF16)
            s_wvT = psb("s_wvT", [P, 8 * D], BF16)
            s_wqT = psb("s_wqT", [P, 8 * D], BF16)
            s_wqpk = psb("s_wqpk", [P, 8 * 512], BF16)
            s_keysT = psb("s_keysT", [64, 512], BF16)
            s_bkB = psb("s_bkB", [P, D], F32)
            s_bvB = psb("s_bvB", [P, D], F32)
            s_bqc = psb("s_bqc", [64, 16], F32)
            s_bkc = psb("s_bkc", [64, 16], F32)
            s_x = [psb(f"s_x{i}", [P, D], F32) for i in range(2)]
            s_xnbf = [psb(f"s_xnbf{i}", [P, D], BF16) for i in range(2)]
            s_xnTg = [psb(f"s_xnTg{i}", [P, 8 * G], BF16) for i in range(2)]
            s_K = [psb(f"s_K{i}", [P, D], BF16) for i in range(2)]
            s_V = [psb(f"s_V{i}", [P, D], BF16) for i in range(2)]
            s_mu = [psb(f"s_mu{i}", [P, 1], F32) for i in range(2)]
            s_sq = psb("s_sq", [P, D], F32)
            s_nmu = [psb(f"s_nmu{i}", [P, 1], F32) for i in range(2)]
            s_var = [psb(f"s_var{i}", [P, 1], F32) for i in range(2)]
            s_rstd = [psb(f"s_rstd{i}", [P, 1], F32) for i in range(2)]
            s_QTg = psb("s_QTg", [64, NH * G], BF16)
            s_KTg = psb("s_KTg", [64, NH * G], BF16)
            s_qsT = psb("s_qsT", [64, 8 * G], BF16)
            s_simall = psb("s_simall", [P, 512], F32)
            s_sxA = psb("s_sxA", [P, 32], F32)
            s_syA = psb("s_syA", [P, 32], F32)
            s_mA = psb("s_mA", [P, 4], F32)
            s_nmA = psb("s_nmA", [P, 4], F32)
            s_markx = psb("s_markx", [P, 4 * NK], F32)
            s_marky = psb("s_marky", [P, 4 * NK], F32)
            s_maskx = psb("s_maskx", [P, 4 * NK], F32)
            s_masky = psb("s_masky", [P, 4 * NK], F32)
            s_ex = psb("s_ex", [P, 4 * NK], F32)
            s_ey = psb("s_ey", [P, 4 * NK], F32)
            s_eexs = psb("s_eexs", [P, 4 * NK], BF16)
            s_eeys = psb("s_eeys", [P, 4 * NK], BF16)
            s_exs = psb("s_exs", [P, 32], F32)
            s_eys = psb("s_eys", [P, 32], F32)
            s_exsb = psb("s_exsb", [P, 32], BF16)
            s_eysb = psb("s_eysb", [P, 32], BF16)
            s_allE = psb("s_allE", [P, 4 * NK], BF16)
            s_top8all = psb("s_top8all", [P, 32 * NT], F32)
            s_Z = psb("s_Z", [P, 4], F32)

            for i in range(8):
                b.dma("gpsimd", s_wkT[:, D * i:D * (i + 1)], wkT.rearrange("(i p) e -> i p e", p=P)[i], "ldg")
                b.dma("gpsimd", s_wvT[:, D * i:D * (i + 1)], wvT.rearrange("(i p) e -> i p e", p=P)[i], "ldg")
                b.dma("gpsimd", s_wqT[:, D * i:D * (i + 1)], wqT.rearrange("(i p) e -> i p e", p=P)[i], "ldg")
                b.dma("gpsimd", s_wqpk[:, 512 * i:512 * (i + 1)], WqT_pk.rearrange("(i p) e -> i p e", p=P)[i], "ldg")
            b.dma("gpsimd", s_keysT[:], keysT, "ldg")
            b.dma("sync", s_bkB[:], bkB, "ld")
            b.dma("sync", s_bvB[:], bvB, "ld")
            b.dma("sync", s_bqc[:], bq_col, "ld")
            b.dma("sync", s_bkc[:], bk_col, "ld")
            for eng in ("tensor", "vector", "scalar"):
                b.wait(eng, "ldg")
                b.wait(eng, "ld")
            b.wait("gpsimd", "ldg")

            def ln_tile(buf):
                VE.reduce_sum(s_mu[buf][:], s_x[buf][:], axis=AX.X)
                VE.tensor_scalar_mul(s_nmu[buf][:], s_mu[buf][:], -1.0 / D)
                VE.tensor_scalar_mul(s_mu[buf][:], s_mu[buf][:], 1.0 / D, inc="v_mu")
                b.wait("scalar", "v_mu")
                SC.activation(s_sq[:], s_x[buf][:], func=AF.Square, bias=s_nmu[buf][:], scale=1.0,
                              accum_out=s_var[buf][:], inc="sc_var")
                b.wait("vector", "sc_var")
                VE.tensor_scalar(s_var[buf][:], s_var[buf][:], 1.0 / D, 1e-5,
                                 op0=ALU.mult, op1=ALU.add, inc="v_var")
                b.wait("scalar", "v_var")
                SC.activation(s_rstd[buf][:], s_var[buf][:], func=AF.Ln)
                SC.activation(s_rstd[buf][:], s_rstd[buf][:], func=AF.Exp, scale=-0.5, inc="sc_rstd")
                b.wait("vector", "sc_rstd")
                VE.tensor_scalar(s_xnbf[buf][:], s_x[buf][:], s_mu[buf][:], s_rstd[buf][:],
                                 op0=ALU.subtract, op1=ALU.mult, inc="v_xn")

            def transpose_tile(buf, gbuf, ti):
                b.wait("tensor", "v_xn")
                if True:
                    b.wait("tensor", "v_xnT_evac")
                for i in range(8):
                    TE.transpose(pbt[:, P * i:P * (i + 1)], s_xnbf[buf][:, P * i:P * (i + 1)], c_ident[:],
                                 inc="t_tp" if i == 7 else None)
                b.wait("vector", "t_tp")
                dst = s_xnTg[gbuf][:].rearrange("p (k t) -> p k t", k=8)[:, :, P * ti:P * (ti + 1)]
                VE.tensor_copy(dst, pbt[:].rearrange("p (k t) -> p k t", k=8), inc="v_xnT_evac")

            def kv_tile(gbuf, ti, kvbuf, do_spill, g):
                b.wait("tensor", "v_kv")
                if do_spill:
                    b.wait("vector", f"sp_kv{kvbuf}")
                for which, wT, bB, dst, scr in ((0, s_wkT, s_bkB, s_K, scr_Kn), (1, s_wvT, s_bvB, s_V, scr_V)):
                    bank = pbk[2 * which:2 * which + 2]
                    for n in range(2):
                        for k in range(8):
                            TE.matmul(bank[n][:],
                                      s_xnTg[gbuf][:, G * k + P * ti: G * k + P * (ti + 1)],
                                      wT[:, D * k + 512 * n: D * k + 512 * (n + 1)],
                                      start=(k == 0), stop=(k == 7),
                                      inc="t_kv" if (k == 7) else None)
                    b.wait("vector", "t_kv")
                    for n in range(2):
                        VE.tensor_tensor(dst[kvbuf][:, 512 * n:512 * (n + 1)], bank[n][:],
                                         bB[:, 512 * n:512 * (n + 1)], op=ALU.add,
                                         inc="v_kv" if n == 1 else None)
                    if do_spill:
                        b.wait("sync", "v_kv")
                        b.dma("sync", scr[g, P * ti:P * (ti + 1), :], dst[kvbuf][:], f"sp_kv{kvbuf}")

            for g in range(NG if max_pass >= 0 else 0):
                for ti in range(TPG):
                    t_abs = g * TPG + ti
                    buf = t_abs % 2
                    gbuf = g % 2
                    if b.cnt.get("v_xn", 0) >= 2:
                        b.wait("sync", "v_xn", b.cnt["v_xn"] - 1)
                    ldx = b.dma("sync", s_x[buf][:], x_pre.rearrange("(n p) d -> n p d", p=P)[t_abs], f"ldx{buf}")
                    b.wait("vector", f"ldx{buf}", ldx)
                    b.wait("scalar", f"ldx{buf}", ldx)
                    ln_tile(buf)
                    transpose_tile(buf, gbuf, ti)
                    b.wait("tensor", "v_xnT_evac")
                    kv_tile(gbuf, ti, buf, False, g)
                    b.wait("tensor", "v_kv")
                    for h in range(NH):
                        TE.matmul(pbk[5 + h // 8][0:64, 64 * (h % 8):64 * (h % 8) + 64],
                                  s_K[buf][:, DH * h:DH * (h + 1)],
                                  s_V[buf][:, DH * h:DH * (h + 1)],
                                  start=(t_abs == 0 and h % 8 == 0),
                                  stop=(t_abs == NT - 1 and h % 8 == 7),
                                  skip_group_check=True,
                                  inc="t_state" if h == NH - 1 else None)
            if max_pass >= 0:
                b.wait("vector", "t_state")
                VE.tensor_scalar(s_S[0:64, 0:512], pbk[5][0:64, :], c_presc[0:64, :], None,
                                 op0=ALU.mult)
                VE.tensor_scalar(s_S[0:64, 512:1024], pbk[6][0:64, :], c_presc[0:64, :], None,
                                 op0=ALU.mult, inc="v_sready")
            if debug and b.cnt.get("v_sready"):
                b.wait("sync", "v_sready")
                b.dma("sync", scr_S, s_S[0:64, :], "dbg")

            b.barrier("p0")

            for g in range(NG if max_pass >= 1 else 0):
                gbuf = g % 2
                b.wait("vector", f"sp_xnT{gbuf}")
                b.wait("vector", "sp_qk")
                for ti in range(TPG):
                    t_abs = g * TPG + ti
                    buf = t_abs % 2
                    if b.cnt.get("v_xn", 0) >= 2:
                        b.wait("sync", "v_xn", b.cnt["v_xn"] - 1)
                    ldx = b.dma("sync", s_x[buf][:], x_own.rearrange("(n p) d -> n p d", p=P)[t_abs], f"ldx{buf}")
                    b.wait("vector", f"ldx{buf}", ldx)
                    b.wait("scalar", f"ldx{buf}", ldx)
                    ln_tile(buf)
                    transpose_tile(buf, gbuf, ti)
                    b.wait("tensor", "v_xnT_evac")
                    kv_tile(gbuf, ti, buf, True, g)
                if p1_stage < 1:
                    continue
                b.wait("sync", "v_xnT_evac")
                b.dma("sync", scr_xnT[g].rearrange("(k p) t -> p k t", p=P),
                      s_xnTg[gbuf][:].rearrange("p (k t) -> p k t", k=8), f"sp_xnT{gbuf}")
                if p1_stage < 2:
                    continue
                for which, wT, bcol, dstT, scr in ((0, s_wqT, s_bqc, s_QTg, scr_QT), (1, s_wkT, s_bkc, s_KTg, scr_KT)):
                    for h in range(NH):
                        bank = pbk[h % 2]
                        for k in range(8):
                            TE.matmul(bank[0:64, :],
                                      wT[:, D * k + DH * h: D * k + DH * (h + 1)],
                                      s_xnTg[gbuf][:, G * k:G * (k + 1)],
                                      start=(k == 0), stop=(k == 7),
                                      skip_group_check=True,
                                      inc="t_qk" if k == 7 else None)
                        b.wait("vector", "t_qk")
                        VE.tensor_scalar(dstT[0:64, G * h:G * (h + 1)], bank[0:64, :],
                                         bcol[:, h:h + 1], None,
                                         op0=ALU.add, inc="v_qk")
                        b.wait("tensor", "v_qk", b.cnt["v_qk"] - 1)
                    b.wait("sync", "v_qk")
                    b.dma("sync", scr[g], dstT[0:64, :], "sp_qk")
                if p1_stage < 3:
                    continue
                b.wait("scalar", "t_sim")
                for j in range(8):
                    bank = pbk[2 + (j % 2)]
                    sub = j % 2 * 0
                    for k in range(8):
                        TE.matmul(bank[0:64, :],
                                  s_wqpk[:, 512 * k + 64 * j: 512 * k + 64 * (j + 1)],
                                  s_xnTg[gbuf][:, G * k:G * (k + 1)],
                                  start=(k == 0), stop=(k == 7),
                                  skip_group_check=True,
                                  inc="t_qs" if k == 7 else None)
                    b.wait("scalar", "t_qs")
                    SC.copy(s_qsT[0:64, G * j:G * (j + 1)], bank[0:64, :], inc="sc_qs")
                    b.wait("tensor", "sc_qs")
                for ti in range(TPG if p1_stage >= 4 else 0):
                    t_abs = g * TPG + ti
                    b.wait("tensor", "v_simevac")
                    for p_ in range(2):
                        for h in range(4):
                            jj = p_ * 4 + h
                            TE.matmul(pbk[4][:, (p_ * 4 + h) * 64:(p_ * 4 + h) * 64 + 64],
                                      s_qsT[0:64, G * jj + P * ti:G * jj + P * (ti + 1)],
                                      s_keysT[0:64, (p_ * 4 + h) * 64:(p_ * 4 + h) * 64 + 64],
                                      start=True, stop=True, skip_group_check=True,
                                      inc="t_sim" if (p_ == 1 and h == 3) else None)
                    b.wait("vector", "t_sim")
                    b.wait("vector", "sc_exp")
                    b.wait("vector", "sp_ee")
                    VE.tensor_copy(s_simall[:], pbk[4][:], inc="v_simevac")
                    NH4 = range(4 if p1_stage >= 5 else 0)
                    rounds = 2 if t_abs == 0 else 1
                    for rnd in range(rounds):
                        if rnd == 1:
                            b.wait("vector", "sc_exp")
                        xs = lambda h: slice(h * 64, h * 64 + 64)
                        ys = lambda h: slice(256 + h * 64, 256 + h * 64 + 64)
                        for h in NH4:
                            VE.max(out=s_sxA[:, 8 * h:8 * h + 8], in_=s_simall[:, xs(h)])
                            VE.max(out=s_syA[:, 8 * h:8 * h + 8], in_=s_simall[:, ys(h)],
                                   inc="v_max" if h == 3 else None)
                        b.wait("vector", "v_max")
                        for h in NH4:
                            VE.match_replace(out=s_markx[:, 64 * h:64 * h + 64],
                                             in_to_replace=s_sxA[:, 8 * h:8 * h + 8],
                                             in_values=s_simall[:, xs(h)], imm_value=1e30)
                            VE.match_replace(out=s_marky[:, 64 * h:64 * h + 64],
                                             in_to_replace=s_syA[:, 8 * h:8 * h + 8],
                                             in_values=s_simall[:, ys(h)], imm_value=1e30,
                                             inc="v_mr" if h == 3 else None)
                        b.wait("vector", "v_mr")
                        for h in NH4:
                            VE.tensor_tensor(s_mA[:, h:h + 1], s_sxA[:, 8 * h:8 * h + 1],
                                             s_syA[:, 8 * h:8 * h + 1], op=ALU.add)
                        for h in NH4:
                            VE.tensor_scalar_mul(s_nmA[:, h:h + 1], s_mA[:, h:h + 1], -1.0,
                                                 inc="v_m" if h == 3 else None)
                        for h in NH4:
                            VE.tensor_tensor(s_maskx[:, 64 * h:64 * h + 64], s_markx[:, 64 * h:64 * h + 64],
                                             s_simall[:, xs(h)], op=ALU.not_equal)
                            VE.tensor_tensor(s_masky[:, 64 * h:64 * h + 64], s_marky[:, 64 * h:64 * h + 64],
                                             s_simall[:, ys(h)], op=ALU.not_equal,
                                             inc="v_mask" if h == 3 else None)
                        b.wait("scalar", "v_m")
                        for h in NH4:
                            SC.activation(s_ex[:, 64 * h:64 * h + 64], s_simall[:, xs(h)], func=AF.Exp,
                                          bias=s_nmA[:, h:h + 1], scale=1.0)
                            SC.activation(s_ey[:, 64 * h:64 * h + 64], s_simall[:, ys(h)], func=AF.Exp,
                                          scale=1.0)
                            SC.activation(s_exs[:, 8 * h:8 * h + 8], s_sxA[:, 8 * h:8 * h + 8], func=AF.Exp,
                                          bias=s_nmA[:, h:h + 1], scale=1.0)
                            SC.activation(s_eys[:, 8 * h:8 * h + 8], s_syA[:, 8 * h:8 * h + 8], func=AF.Exp,
                                          scale=1.0, inc="sc_exp" if h == 3 else None)
                        b.wait("vector", "sc_exp")
                        b.wait("vector", "v_mask")
                        for h in NH4:
                            VE.tensor_mul(s_eexs[:, 64 * h:64 * h + 64], s_ex[:, 64 * h:64 * h + 64],
                                          s_maskx[:, 64 * h:64 * h + 64])
                            VE.tensor_mul(s_eeys[:, 64 * h:64 * h + 64], s_ey[:, 64 * h:64 * h + 64],
                                          s_masky[:, 64 * h:64 * h + 64], inc="v_ee" if h == 3 else None)
                        VE.tensor_copy(s_exsb[:], s_exs[:])
                        VE.tensor_copy(s_eysb[:], s_eys[:])
                        for h in NH4:
                            VE.tensor_tensor(
                                s_allE[:, 64 * h:64 * h + 64].rearrange("p (a c) -> p a c", a=8),
                                s_exsb[:, 8 * h:8 * h + 8].rearrange("p (a c) -> p a c", c=1).to_broadcast([P, 8, 8]),
                                s_eysb[:, 8 * h:8 * h + 8].rearrange("p (a c) -> p a c", a=1).to_broadcast([P, 8, 8]),
                                op=ALU.mult, inc="v_alle" if h == 3 else None)
                        b.wait("vector", "v_alle")
                        for h in NH4:
                            VE.max(out=s_top8all[:, 32 * t_abs + 8 * h:32 * t_abs + 8 * h + 8],
                                   in_=s_allE[:, 64 * h:64 * h + 64],
                                   inc="v_max2" if h == 3 else None)
                    b.wait("sync", "v_ee")
                    b.dma("sync", scr_EEx[t_abs], s_eexs[:], "sp_ee")
                    b.dma("sync", scr_EEy[t_abs], s_eeys[:], "sp_ee")
            if max_pass >= 1 and p1_stage >= 5:
                for _ in range(4):
                    VE.max(out=s_wu8[:], in_=s_wu64[:])
                b.barrier("preT")
                for tp in range(NT):
                    for h in range(4):
                        VE.reduce_sum(s_Z[:, h:h + 1], s_top8all[:, 32 * tp + 8 * h:32 * tp + 8 * h + 8],
                                      axis=AX.X)
                    for h in range(4):
                        VE.tensor_copy(s_T[:, tp * 4 + h:tp * 4 + h + 1],
                                       s_top8all[:, 32 * tp + 8 * h + 7:32 * tp + 8 * h + 8])
                    for h in range(4):
                        VE.reciprocal(s_rZ[:, tp * 4 + h:tp * 4 + h + 1], s_Z[:, h:h + 1],
                                      inc="v_rz" if h == 3 else None)
            if debug and b.cnt.get("v_rz"):
                b.wait("sync", "v_rz")
                b.dma("sync", scr_T, s_T[:], "dbg")
                b.dma("sync", scr_rZ, s_rZ[:], "dbg")

            for snm in ("sp_xnT0", "sp_xnT1", "sp_qk", "sp_kv0", "sp_kv1", "sp_ee"):
                if b.cnt.get(snm):
                    b.wait("sync", snm)
            b.barrier("p1")

        with ExitStack() as ps:
            def psb(name, shape, dtype):
                return ps.enter_context(nc.sbuf_tensor(name, shape, dtype))

            s_WupT = psb("s_WupT", [P, 8 * I_FFN], BF16)
            s_WdnT = psb("s_WdnT", [P, 32 * D], BF16)
            s_xnTg = [psb(f"p2_xnT{i}", [P, 8 * G], BF16) for i in range(2)]
            s_sup = psb("s_sup", [P, 32 * G], BF16)
            s_dense = [psb(f"s_dense{i}", [P, D], F32) for i in range(2)]

            for i in range(8):
                b.dma("gpsimd", s_WupT[:, I_FFN * i:I_FFN * (i + 1)],
                      WupT.rearrange("(i p) e -> i p e", p=P)[i], "ldg")
            for i in range(32):
                b.dma("gpsimd", s_WdnT[:, D * i:D * (i + 1)],
                      WdownT.rearrange("(i p) e -> i p e", p=P)[i], "ldg")
            for eng in ("tensor", "vector", "scalar"):
                b.wait(eng, "ldg")

            for g in range(NG if max_pass >= 2 else 0):
                gbuf = g % 2
                if g >= 2:
                    b.wait("sync", "t_up")
                b.dma("sync", s_xnTg[gbuf][:].rearrange("p (k t) -> p k t", k=8),
                      scr_xnT[g].rearrange("(k p) t -> p k t", p=P), f"ld_xnT{gbuf}")
                b.wait("tensor", f"ld_xnT{gbuf}")
                b.wait("scalar", "t_dn")
                for m in range(32):
                    bank = pbk[m % 2]
                    if m >= 2:
                        b.wait("tensor", "sc_sup", b.cnt["sc_sup"] - 1)
                    for k in range(8):
                        TE.matmul(bank[:],
                                  s_WupT[:, I_FFN * k + P * m: I_FFN * k + P * (m + 1)],
                                  s_xnTg[gbuf][:, G * k:G * (k + 1)],
                                  start=(k == 0), stop=(k == 7),
                                  inc="t_up" if k == 7 else None)
                    b.wait("scalar", "t_up")
                    SC.activation(s_sup[:, G * m:G * (m + 1)], bank[:], func=AF.Silu, inc="sc_sup")
                b.wait("tensor", "sc_sup")
                for ti in range(TPG):
                    t_abs = g * TPG + ti
                    dbuf = ti % 2
                    banks = pbk[2 + 2 * dbuf: 4 + 2 * dbuf]
                    if b.cnt.get("v_dense", 0) >= 2:
                        b.wait("tensor", "v_dense", b.cnt["v_dense"] - 1)
                    b.wait("vector", f"sp_acc{dbuf}")
                    for m in range(32):
                        for n in range(2):
                            TE.matmul(banks[n][:],
                                      s_sup[:, G * m + P * ti: G * m + P * (ti + 1)],
                                      s_WdnT[:, D * m + 512 * n: D * m + 512 * (n + 1)],
                                      start=(m == 0), stop=(m == 31),
                                      inc="t_dn" if (m == 31 and n == 1) else None)
                    b.wait("vector", "t_dn")
                    for n in range(2):
                        VE.tensor_tensor(s_dense[dbuf][:, 512 * n:512 * (n + 1)], banks[n][:],
                                         s_woB[:, 512 * n:512 * (n + 1)], op=ALU.add,
                                         inc="v_dense" if n == 1 else None)
                    b.wait("sync", "v_dense")
                    b.dma("sync", scr_acc.rearrange("(n p) d -> n p d", p=P)[t_abs], s_dense[dbuf][:], f"sp_acc{dbuf}")

            if b.cnt.get("sp_acc0"):
                b.wait("sync", "sp_acc0")
            if b.cnt.get("sp_acc1"):
                b.wait("sync", "sp_acc1")
            b.barrier("p2")

        with ExitStack() as ps:
            def psb(name, shape, dtype):
                return ps.enter_context(nc.sbuf_tensor(name, shape, dtype))

            HE = NE // 2
            s_deT = psb("s_deT", [P, 8 * HE], BF16)
            s_ue = psb("s_ue", [P, 16 * D], BF16)
            s_xnTg = [psb(f"p3_xnT{i}", [P, 8 * G], BF16) for i in range(2)]
            s_eex = [psb(f"p3_eex{i}", [P, 4 * NK], BF16) for i in range(2)]
            s_eey = [psb(f"p3_eey{i}", [P, 4 * NK], BF16) for i in range(2)]
            s_L = [psb(f"s_L{i}", [P, HE], BF16) for i in range(2)]
            s_E = [psb(f"s_E{i}", [P, HE], BF16) for i in range(2)]
            s_m01 = psb("s_m01", [P, HE], BF16)
            s_Pm = psb("s_Pm", [P, HE], BF16)
            s_q = psb("s_q", [P, HE], BF16)
            s_Wa = [psb(f"s_Wa{i}", [P, HE], BF16) for i in range(2)]
            s_Wtmp = psb("s_Wtmp", [P, HE], BF16)
            s_Wt = psb("s_Wt", [P, HE], BF16)
            s_es = [psb(f"s_es{i}", [P, D], F32) for i in range(2)]

            stashA = [None]
            stashB = [None]

            def flush_A():
                fn = stashA[0]
                stashA[0] = None
                if fn:
                    fn()

            def flush_B():
                fn = stashB[0]
                stashB[0] = None
                if fn:
                    fn()

            def flush_stash():
                flush_A()
                flush_B()

            job = [0]
            for he in range(2 if max_pass >= 3 else 0):
                flush_stash()
                b.wait("gpsimd", "sp_es0")
                b.wait("gpsimd", "sp_es1")
                if he == 1:
                    b.wait("gpsimd", "t_es")
                for i in range(8):
                    b.dma("gpsimd", s_deT[:, HE * i:HE * (i + 1)],
                          deT.rearrange("(i p) e -> i p e", p=P)[i, :, HE * he:HE * (he + 1)], "ldg")
                for i in range(16):
                    b.dma("gpsimd", s_ue[:, D * i:D * (i + 1)],
                          ue_n.rearrange("(i p) e -> i p e", p=P)[16 * he + i], "ldg")
                for eng in ("tensor", "vector", "scalar", "gpsimd"):
                    b.wait(eng, "ldg")
                for g in range(NG):
                    gbuf = g % 2
                    b.wait("sync", "t_L")
                    b.dma("sync", s_xnTg[gbuf][:].rearrange("p (k t) -> p k t", k=8),
                          scr_xnT[g].rearrange("(k p) t -> p k t", p=P), f"ld_xnT{gbuf}")
                    b.wait("tensor", f"ld_xnT{gbuf}")
                    for ti in range(TPG):
                        t_abs = g * TPG + ti
                        j = job[0]
                        p2 = j % 2
                        if b.cnt.get("g_E", 0) >= 8:
                            b.wait("sync", "g_E", b.cnt["g_E"] - 4)
                        ldee = b.dma("sync", s_eex[p2][:], scr_EEx[t_abs], "ld_ee")
                        ldee = b.dma("sync", s_eey[p2][:], scr_EEy[t_abs], "ld_ee")
                        if b.cnt.get("v_L", 0):
                            b.wait("tensor", "v_L")
                        lbanks = (pbk[0], pbk[1], pbk[5], pbk[6])
                        for k in range(8):
                            for c in range(4):
                                TE.matmul(lbanks[c][:],
                                          s_xnTg[gbuf][:, G * k + P * ti: G * k + P * (ti + 1)],
                                          s_deT[:, HE * k + 512 * c: HE * k + 512 * (c + 1)],
                                          start=(k == 0), stop=(k == 7),
                                          inc="t_L" if (k == 7 and c == 3) else None)
                        b.wait("vector", "t_L")
                        for c in range(4):
                            VE.tensor_copy(s_L[p2][:, 512 * c:512 * (c + 1)], lbanks[c][:],
                                           inc="v_L")
                        flush_A()
                        b.wait("gpsimd", "ld_ee", ldee)
                        for h in range(4):
                            tcol = slice(t_abs * 4 + h, t_abs * 4 + h + 1)
                            eb = h % 2
                            exsl = s_eex[p2][:].rearrange("p (h n) -> p h n", h=4)[:, h, 32 * he:32 * he + 32]
                            eysl = s_eey[p2][:].rearrange("p (h n) -> p h n", h=4)[:, h]
                            if b.cnt.get("v_Pm", 0) >= 2:
                                b.wait("gpsimd", "v_Pm", b.cnt["v_Pm"] - 1)
                            GP.tensor_tensor(
                                s_E[eb][:].rearrange("p (a c) -> p a c", a=32),
                                exsl.rearrange("p (a c) -> p a c", c=1).to_broadcast([P, 32, NK]),
                                eysl.rearrange("p (a c) -> p a c", a=1).to_broadcast([P, 32, NK]),
                                op=ALU.mult, inc="g_E")
                            b.wait("vector", "g_E")
                            VE.tensor_single_scalar(s_m01[:], s_E[eb][:], s_T[:, tcol], op=ALU.is_ge)
                            VE.tensor_mul(s_Pm[:], s_m01[:], s_E[eb][:], inc="v_Pm")
                            VE.tensor_scalar_mul(s_q[:], s_Pm[:], s_rZ[:, tcol])
                            VE.tensor_mul(s_q[:], s_q[:], s_L[p2][:], inc="v_q")
                            b.wait("scalar", "v_q")
                            if h == 0:
                                SC.activation(s_Wa[p2][:], s_q[:], func=AF.Silu, inc="sc_W")
                            else:
                                SC.activation(s_Wtmp[:], s_q[:], func=AF.Silu, inc="sc_W")
                                b.wait("vector", "sc_W")
                                VE.tensor_add(s_Wa[p2][:], s_Wa[p2][:], s_Wtmp[:], inc="v_Wacc")

                        v_scW = b.cnt["sc_W"]
                        v_Wacc = b.cnt["v_Wacc"]
                        v_tes_prev = b.cnt.get("t_es", 0)

                        def mk(j=j, p2=p2, t_abs=t_abs, v_scW=v_scW, v_Wacc=v_Wacc,
                               v_tes_prev=v_tes_prev):
                            def go():
                                ep = j % 2
                                b.wait("tensor", "sc_W", v_scW)
                                b.wait("tensor", "v_Wacc", v_Wacc)
                                b.wait("scalar", "t_es", v_tes_prev)
                                for r in range(2):
                                    b.wait("tensor", "sc_Wt")
                                    for i in range(8):
                                        TE.transpose(pbt[:, P * i:P * (i + 1)],
                                                     s_Wa[p2][:, P * (8 * r + i):P * (8 * r + i + 1)],
                                                     c_ident[:],
                                                     inc="t_Wt" if i == 7 else None)
                                    b.wait("scalar", "t_Wt")
                                    SC.copy(s_Wt[:, P * 8 * r:P * 8 * (r + 1)], pbt[:], inc="sc_Wt")
                                b.wait("tensor", "sc_Wt")
                                banks = pbk[2:4]
                                if j >= 1:
                                    b.wait("tensor", "v_es", j)
                                for m in range(16):
                                    for n in range(2):
                                        TE.matmul(banks[n][:],
                                                  s_Wt[:, P * m:P * (m + 1)],
                                                  s_ue[:, D * m + 512 * n: D * m + 512 * (n + 1)],
                                                  start=(m == 0), stop=(m == 15),
                                                  inc="t_es" if (m == 15 and n == 1) else None)
                            return go

                        def mkB(j=j, t_abs=t_abs):
                            def go():
                                ep = j % 2
                                banks = pbk[2:4]
                                b.wait("vector", "t_es")
                                b.wait("vector", f"sp_es{ep}")
                                for n in range(2):
                                    VE.tensor_copy(s_es[ep][:, 512 * n:512 * (n + 1)], banks[n][:],
                                                   inc="v_es" if n == 1 else None)
                                b.wait("gpsimd", "v_es")
                                b.dma("gpsimd", scr_acc.rearrange("(n p) d -> n p d", p=P)[t_abs],
                                      s_es[ep][:], f"sp_es{ep}", accum_op=ALU.add)
                            return go

                        flush_B()
                        stashA[0] = mk()
                        stashB[0] = mkB()
                        job[0] += 1
            flush_stash()

            if b.cnt.get("sp_es0"):
                b.wait("gpsimd", "sp_es0")
            if b.cnt.get("sp_es1"):
                b.wait("gpsimd", "sp_es1")
            b.barrier("p3")

        with ExitStack() as ps:
            def psb(name, shape, dtype):
                return ps.enter_context(nc.sbuf_tensor(name, shape, dtype))

            s_woT = psb("s_woT", [P, 8 * D], BF16)
            s_QT = psb("s_QT", [64, NH * G], BF16)
            s_KT = psb("s_KT", [64, NH * G], BF16)
            s_Kn = psb("s_Kn", [P, 4 * D], BF16)
            s_Vn = psb("s_Vn", [P, 4 * D], BF16)
            s_sT = psb("s_sT", [P, 16 * P], BF16)
            s_AT = psb("s_AT", [P, 8 * G], BF16)
            s_xf = [psb(f"s_xf{i}", [P, D], F32) for i in range(2)]
            s_af = [psb(f"s_af{i}", [P, D], F32) for i in range(2)]
            s_out = [psb(f"s_out{i}", [P, D], F32) for i in range(2)]

            for i in range(8):
                b.dma("gpsimd", s_woT[:, D * i:D * (i + 1)], woT.rearrange("(i p) e -> i p e", p=P)[i], "ldg")
            for eng in ("tensor", "vector"):
                b.wait(eng, "ldg")
            VE.tensor_copy(s_Sbf[0:64, :], s_S[0:64, :], inc="v_sbf")
            b.wait("tensor", "v_sbf")

            for g in range(NG if max_pass >= 4 else 0):
                b.wait("sync", "t_O")
                b.wait("vector", "t_O")
                b.dma("sync", s_QT[0:64, :], scr_QT[g], "ld_g4")
                b.dma("sync", s_KT[0:64, :], scr_KT[g], "ld_g4")
                b.dma("sync", s_Kn[:].rearrange("p (c d) -> p c d", c=4),
                      scr_Kn[g].rearrange("(c p) d -> p c d", p=P), "ld_g4")
                b.dma("sync", s_Vn[:].rearrange("p (c d) -> p c d", c=4),
                      scr_V[g].rearrange("(c p) d -> p c d", p=P), "ld_g4")
                b.wait("tensor", "ld_g4")
                for ti in range(TPG):
                    t_abs = g * TPG + ti
                    b.wait("tensor", "v_out")
                    for r in range(2):
                        b.wait("tensor", "v_sT")
                        for hh in range(8):
                            h = 8 * r + hh
                            TE.matmul(pbk[hh // 4][:, P * (hh % 4):P * (hh % 4 + 1)],
                                      s_KT[0:64, G * h + P * ti: G * h + P * (ti + 1)],
                                      s_QT[0:64, G * h + P * ti: G * h + P * (ti + 1)],
                                      start=True, stop=True, skip_group_check=True,
                                      inc="t_sc" if hh == 7 else None)
                        b.wait("vector", "t_sc")
                        for bk in range(2):
                            VE.tensor_tensor(
                                s_sT[:].rearrange("p (h t) -> p h t", h=16)[:, 8 * r + 4 * bk:8 * r + 4 * bk + 4],
                                pbk[bk][:].rearrange("p (h t) -> p h t", h=4),
                                c_triu[:].rearrange("p (o t) -> p o t", o=1).to_broadcast([P, 4, P]),
                                op=ALU.mult, inc="v_sT" if bk == 1 else None)
                    b.wait("tensor", "v_sT")
                    b.wait("tensor", "v_AT")
                    b.wait("tensor", "v_sbf")
                    for h in range(NH):
                        hp = h // 2
                        bank = pbk[2 + hp // 4]
                        reg = bank[64 * (h % 2):64 * (h % 2) + 64, P * (hp % 4):P * (hp % 4) + P]
                        TE.matmul(reg, s_Vn[:, D * ti + DH * h: D * ti + DH * (h + 1)],
                                  s_sT[:, P * h:P * (h + 1)],
                                  start=True, stop=False, tile_position=(0, 64 * (h % 2)),
                                  skip_group_check=True)
                        TE.matmul(reg, s_Sbf[0:64, DH * h:DH * (h + 1)],
                                  s_QT[0:64, G * h + P * ti: G * h + P * (ti + 1)],
                                  start=False, stop=True, tile_position=(0, 64 * (h % 2)),
                                  skip_group_check=True,
                                  inc="t_AT" if h == NH - 1 else None)
                    for h in range(NH):
                        TE.matmul(pbk[5 + h // 8][0:64, 64 * (h % 8):64 * (h % 8) + 64],
                                  s_Kn[:, D * ti + DH * h: D * ti + DH * (h + 1)],
                                  s_Vn[:, D * ti + DH * h: D * ti + DH * (h + 1)],
                                  start=True, stop=True,
                                  skip_group_check=True,
                                  inc="t_st" if h == NH - 1 else None)
                    b.wait("vector", "t_AT")
                    for bk in range(2):
                        VE.tensor_copy(
                            s_AT[:].rearrange("p (i t) -> p i t", i=8)[:, 4 * bk:4 * bk + 4, P * ti:P * (ti + 1)],
                            pbk[2 + bk][:].rearrange("p (r t) -> p r t", r=4),
                            inc="v_AT" if bk == 1 else None)
                    b.wait("vector", "t_st")
                    VE.tensor_add(s_S[0:64, 0:512], s_S[0:64, 0:512], pbk[5][0:64, :])
                    VE.tensor_add(s_S[0:64, 512:1024], s_S[0:64, 512:1024], pbk[6][0:64, :])
                    VE.tensor_copy(s_Sbf[0:64, :], s_S[0:64, :], inc="v_sbf")
                b.wait("tensor", "v_AT")
                for ti in range(TPG):
                    t_abs = g * TPG + ti
                    obuf = ti % 2
                    banks = pbk[2 * obuf:2 * obuf + 2]
                    if b.cnt.get("v_out", 0) >= 2:
                        b.wait("sync", "v_out", b.cnt["v_out"] - 1)
                    ldf = b.dma("sync", s_xf[obuf][:], x_own.rearrange("(n p) d -> n p d", p=P)[t_abs], f"ld_fin{obuf}")
                    ldf = b.dma("sync", s_af[obuf][:], scr_acc.rearrange("(n p) d -> n p d", p=P)[t_abs], f"ld_fin{obuf}")
                    if b.cnt.get("v_out", 0) >= 2:
                        b.wait("tensor", "v_out", b.cnt["v_out"] - 1)
                    for i in range(8):
                        for n in range(2):
                            TE.matmul(banks[n][:],
                                      s_AT[:, G * i + P * ti: G * i + P * (ti + 1)],
                                      s_woT[:, D * i + 512 * n: D * i + 512 * (n + 1)],
                                      start=(i == 0), stop=(i == 7),
                                      inc="t_O" if (i == 7 and n == 1) else None)
                    b.wait("vector", "t_O")
                    b.wait("vector", f"ld_fin{obuf}", ldf)
                    b.wait("vector", f"st_out{obuf}")
                    for n in range(2):
                        VE.tensor_tensor(s_out[obuf][:, 512 * n:512 * (n + 1)], banks[n][:],
                                         s_af[obuf][:, 512 * n:512 * (n + 1)], op=ALU.add)
                    VE.tensor_add(s_out[obuf][:], s_out[obuf][:], s_xf[obuf][:], inc="v_out")
                    b.wait("sync", "v_out")
                    b.dma("sync", out_d.rearrange("(n p) d -> n p d", p=P)[t_abs], s_out[obuf][:], f"st_out{obuf}")

            if b.cnt.get("st_out0"):
                b.wait("sync", "st_out0")
                b.wait("sync", "st_out1")

        b.run(block)

    return nc


def _host_inputs(inputs):
    f32 = np.float32
    x = np.asarray(inputs["L0_attn_x"], f32)
    wq = np.asarray(inputs["L0_attn_wq_w"], f32); bq = np.asarray(inputs["L0_attn_wq_b"], f32)
    wk = np.asarray(inputs["L0_attn_wk_w"], f32); bk = np.asarray(inputs["L0_attn_wk_b"], f32)
    wv = np.asarray(inputs["L0_attn_wv_w"], f32); bv = np.asarray(inputs["L0_attn_wv_b"], f32)
    wo = np.asarray(inputs["L0_attn_wo_w"], f32); bo = np.asarray(inputs["L0_attn_wo_b"], f32)
    Wup = np.asarray(inputs["L0_ffn_W_up"], f32)
    Wdown = np.asarray(inputs["L0_ffn_W_down"], f32)
    Wq_pk = np.asarray(inputs["L0_ffn_W_q"], f32)
    keys = np.asarray(inputs["L0_ffn_keys"], f32)
    de = np.asarray(inputs["L0_ffn_down_embed"], f32)
    ue = np.asarray(inputs["L0_ffn_up_embed"], f32)

    common = {
        "wqT": np.ascontiguousarray(wq.T), "wkT": np.ascontiguousarray(wk.T),
        "wvT": np.ascontiguousarray(wv.T), "woT": np.ascontiguousarray(wo.T),
        "bq_col": np.ascontiguousarray(bq.reshape(16, 64).T), "bk_col": np.ascontiguousarray(bk.reshape(16, 64).T),
        "bkB": np.ascontiguousarray(np.broadcast_to(bk, (P, D))),
        "bvB": np.ascontiguousarray(np.broadcast_to(bv, (P, D))),
        "woB": np.ascontiguousarray(np.broadcast_to(bo, (P, D))),
        "WupT": np.ascontiguousarray(Wup.T), "WdownT": np.ascontiguousarray(Wdown.T),
        "WqT_pk": np.ascontiguousarray(Wq_pk.T),
        "keysT": np.ascontiguousarray(keys.transpose(3, 2, 0, 1).reshape(64, 512)),
        "deT": np.ascontiguousarray(de.T), "ue_n": np.ascontiguousarray(ue),
        "ident_bf": np.eye(P, dtype=ml_dtypes.bfloat16),
        "triu_f": np.triu(np.ones((P, P), dtype=f32)),
    }
    in_maps = []
    for c in range(8):
        bi, h = c // 2, c % 2
        m = dict(common)
        m["x_own"] = np.ascontiguousarray(x[bi, TOK * h:TOK * (h + 1)])
        m["x_pre"] = np.ascontiguousarray(x[bi, 0:TOK])
        m["pre_scale"] = np.full((P, 1), 1.0 if h == 1 else 0.0, f32)
        in_maps.append(m)
    return in_maps


def kernel(**inputs):
    in_maps = _host_inputs(inputs)
    nc = build_nc()
    res = run_bass_kernel_spmd(nc, in_maps, core_ids=list(range(8)))
    out = np.empty((4, 2 * TOK, D), np.float32)
    for c in range(8):
        bi, h = c // 2, c % 2
        out[bi, TOK * h:TOK * (h + 1)] = np.asarray(res.results[c]["out"], np.float32)
    return out


# version 06
# speedup vs baseline: 1.0657x; 1.0657x over previous
import numpy as np
import ml_dtypes
import concourse.bass as bass
import concourse.mybir as mybir
from concourse.bass_utils import run_bass_kernel_spmd
from contextlib import ExitStack

F32 = mybir.dt.float32
BF16 = mybir.dt.bfloat16
AF = mybir.ActivationFunctionType
ALU = mybir.AluOpType
AX = mybir.AxisListType

P = 128
D = 1024
NH = 16
DH = 64
I_FFN = 4096
NE = 4096
NK = 64
TOK = 4096
G = 512
NG = TOK // G
TPG = G // P
NT = TOK // P
ENGINES = ("sync", "tensor", "vector", "scalar", "gpsimd")


class EngProxy:

    def __init__(self, b, eng):
        self._b = b
        self._eng = eng

    def __getattr__(self, meth):
        b, eng = self._b, self._eng

        def call(*a, inc=None, incn=1, **kw):
            if inc is None:
                b.q[eng].append(lambda e, m=meth, a=a, kw=kw: getattr(e, m)(*a, **kw))
                return None
            h = b.semh(inc)
            b.cnt[inc] += incn
            b.q[eng].append(lambda e, m=meth, a=a, kw=kw, h=h, n=incn: getattr(e, m)(*a, **kw).then_inc(h, n))
            return b.cnt[inc]

        return call


class Builder:
    def __init__(self, nc, es):
        self.nc = nc
        self.es = es
        self.q = {e: [] for e in ENGINES}
        self.sems = {}
        self.cnt = {}
        self.e = {e: EngProxy(self, e) for e in ENGINES}

    def semh(self, name):
        if name not in self.sems:
            self.sems[name] = self.es.enter_context(self.nc.semaphore(name))
            self.cnt[name] = 0
        return self.sems[name]

    def wait(self, eng, name, val=None):
        h = self.semh(name)
        v = self.cnt[name] if val is None else val
        self.q[eng].append(lambda e, h=h, v=v: e.wait_ge(h, v))

    def dma(self, eng, out, in_, sem_name="dma", **kw):
        h = self.semh(sem_name)
        self.cnt[sem_name] += 16
        self.q[eng].append(lambda e, o=out, i=in_, h=h, kw=dict(kw): e.dma_start(out=o, in_=i, **kw).then_inc(h, 16))
        return self.cnt[sem_name]

    def sem_inc(self, eng, name, n=1):
        h = self.semh(name)
        self.cnt[name] += n
        self.q[eng].append(lambda e, h=h, n=n: e.sem_inc(h, n))
        return self.cnt[name]

    def barrier(self, tag):
        name = "bar"
        h = self.semh(name)
        self.cnt[name] += len(ENGINES)
        v = self.cnt[name]
        for eng in ENGINES:
            self.q[eng].append(lambda e, h=h: e.sem_inc(h, 1))
        for eng in ENGINES:
            self.q[eng].append(lambda e, h=h, v=v: e.wait_ge(h, v))

    def run(self, block):
        qs = self.q

        @block.sync
        def _(e):
            for fn in qs["sync"]:
                fn(e)

        @block.tensor
        def _(e):
            for fn in qs["tensor"]:
                fn(e)

        @block.vector
        def _(e):
            for fn in qs["vector"]:
                fn(e)

        @block.scalar
        def _(e):
            for fn in qs["scalar"]:
                fn(e)

        @block.gpsimd
        def _(e):
            for fn in qs["gpsimd"]:
                fn(e)


def build_nc(debug=False, max_pass=99, p1_stage=99):
    nc = bass.Bass("TRN2", target_bir_lowering=False, detect_race_conditions=False)
    dram = {}

    def din(name, shape, dtype=F32):
        dram[name] = nc.dram_tensor(name, shape, dtype, kind="ExternalInput").ap()
        return dram[name]

    x_own = din("x_own", [TOK, D])
    x_pre = din("x_pre", [TOK, D])
    pre_scale = din("pre_scale", [P, 1])
    wqT = din("wqT", [D, D]); wkT = din("wkT", [D, D]); wvT = din("wvT", [D, D]); woT = din("woT", [D, D])
    bq_col = din("bq_col", [P, 8]); bk_col = din("bk_col", [P, 8])
    bkB = din("bkB", [P, D]); bvB = din("bvB", [P, D]); woB = din("woB", [P, D])
    WupT = din("WupT", [D, I_FFN]); WdownT = din("WdownT", [I_FFN, D])
    WqT_pk = din("WqT_pk", [D, 512])
    keysT = din("keysT", [64, 512])
    deT = din("deT", [D, NE]); ue_n = din("ue_n", [NE, D])
    ident_bf = din("ident_bf", [P, P], BF16)
    triu_f = din("triu_f", [P, P])

    out_d = nc.dram_tensor("out", [TOK, D], F32, kind="ExternalOutput").ap()

    skind = "ExternalOutput" if debug else "Internal"

    def scrt(name, shape, dtype):
        return nc.dram_tensor(name, shape, dtype, kind=skind).ap()

    scr_xnT = scrt("scr_xnT", [NG, D, G], BF16)
    scr_QT = scrt("scr_QT", [NG, P, 8 * G], BF16)
    scr_KT = scrt("scr_KT", [NG, P, 8 * G], BF16)
    scr_Kn = scrt("scr_Kn", [NG, G, D], BF16)
    scr_V = scrt("scr_V", [NG, G, D], BF16)
    scr_EEx = scrt("scr_EEx", [NT, P, 4 * NK], BF16)
    scr_EEy = scrt("scr_EEy", [NT, P, 4 * NK], BF16)
    scr_acc = scrt("scr_acc", [TOK, D], F32)
    if debug:
        scr_T = scrt("scr_T", [P, NT * 4], F32)
        scr_rZ = scrt("scr_rZ", [P, NT * 4], F32)
        scr_S = scrt("scr_S", [64, 1024], F32)

    es = ExitStack()
    with es:
        b = Builder(nc, es)
        SY, TE, VE, SC, GP = (b.e[k] for k in ENGINES)

        def sb(name, shape, dtype):
            return es.enter_context(nc.sbuf_tensor(name, shape, dtype))

        c_ident = sb("c_ident", [P, P], BF16)
        c_triu = sb("c_triu", [P, P], F32)
        c_presc = sb("c_presc", [P, 1], F32)
        s_T = sb("s_T", [P, NT * 4], F32)
        s_rZ = sb("s_rZ", [P, NT * 4], F32)
        s_S = sb("s_S", [64, 1024], F32)
        s_Sbf = sb("s_Sbf", [64, 1024], BF16)
        s_woB = sb("s_woB", [P, D], F32)

        pbk = [es.enter_context(nc.psum_tensor(f"pb{i}", [P, 512], F32)) for i in range(7)]
        pbt = es.enter_context(nc.psum_tensor("pbt", [P, 1024], BF16))
        s_wu8 = sb("s_wu8", [P, 8], F32)
        s_wu64 = sb("s_wu64", [P, 64], F32)

        block = es.enter_context(nc.Block())

        b.dma("sync", c_ident[:], ident_bf, "ld")
        b.dma("sync", c_triu[:], triu_f, "ld")
        b.dma("sync", c_presc[:], pre_scale, "ld")
        b.dma("sync", s_woB[:], woB, "ld")
        b.wait("vector", "ld")
        b.wait("scalar", "ld")
        for rnd in range(2):
            VE.tensor_scalar_add(s_wu64[:], c_triu[:, 0:64], 1.0)
            VE.reduce_sum(s_wu8[:, 0:1], s_wu64[:], axis=AX.X)
            VE.scalar_tensor_tensor(s_wu64[:], s_wu64[:], 1.0, s_wu64[:],
                                    op0=ALU.mult, op1=ALU.mult, accum_out=s_wu8[:, 1:2])
            VE.max(out=s_wu8[:], in_=s_wu64[:])
            VE.match_replace(out=s_wu64[:], in_to_replace=s_wu8[:], in_values=s_wu64[:], imm_value=7.0)
            VE.reciprocal(s_wu8[:], s_wu8[:])
            VE.tensor_scalar(s_wu64[:], s_wu64[:], s_wu8[:, 0:1], s_wu8[:, 1:2],
                             op0=ALU.mult, op1=ALU.add, inc="wuv")
            b.wait("scalar", "wuv")
            SC.activation(s_wu64[:], s_wu64[:], func=AF.Square, scale=0.01,
                          accum_out=s_wu8[:, 2:3])
            SC.activation(s_wu64[:], s_wu64[:], func=AF.Ln, scale=0.001)
            SC.activation(s_wu64[:], s_wu64[:], func=AF.Exp, scale=0.001)
            SC.activation(s_wu64[:], s_wu64[:], func=AF.Silu, scale=0.1, inc="wus")
            b.wait("vector", "wus")
        b.barrier("warm")

        with ExitStack() as ps:
            def psb(name, shape, dtype):
                return ps.enter_context(nc.sbuf_tensor(name, shape, dtype))

            s_wkT = psb("s_wkT", [P, 8 * D], BF16)
            s_wvT = psb("s_wvT", [P, 8 * D], BF16)
            s_wqT = psb("s_wqT", [P, 8 * D], BF16)
            s_wqpk = psb("s_wqpk", [P, 8 * 512], BF16)
            s_keysT = psb("s_keysT", [64, 512], BF16)
            s_bkB = psb("s_bkB", [P, D], F32)
            s_bvB = psb("s_bvB", [P, D], F32)
            s_bqc = psb("s_bqc", [P, 8], F32)
            s_bkc = psb("s_bkc", [P, 8], F32)
            s_x = [psb(f"s_x{i}", [P, D], F32) for i in range(2)]
            s_xnbf = [psb(f"s_xnbf{i}", [P, D], BF16) for i in range(2)]
            s_xnTg = [psb(f"s_xnTg{i}", [P, 8 * G], BF16) for i in range(2)]
            s_K = [psb(f"s_K{i}", [P, D], BF16) for i in range(2)]
            s_V = [psb(f"s_V{i}", [P, D], BF16) for i in range(2)]
            s_mu = [psb(f"s_mu{i}", [P, 1], F32) for i in range(2)]
            s_sq = psb("s_sq", [P, D], F32)
            s_nmu = [psb(f"s_nmu{i}", [P, 1], F32) for i in range(2)]
            s_var = [psb(f"s_var{i}", [P, 1], F32) for i in range(2)]
            s_rstd = [psb(f"s_rstd{i}", [P, 1], F32) for i in range(2)]
            s_QTg = psb("s_QTg", [P, 8 * G], BF16)
            s_KTg = psb("s_KTg", [P, 8 * G], BF16)
            s_qsT = psb("s_qsT", [64, 8 * G], BF16)
            s_simall = psb("s_simall", [P, 512], F32)
            s_sxA = psb("s_sxA", [P, 32], F32)
            s_syA = psb("s_syA", [P, 32], F32)
            s_mA = psb("s_mA", [P, 4], F32)
            s_nmA = psb("s_nmA", [P, 4], F32)
            s_markx = psb("s_markx", [P, 4 * NK], F32)
            s_marky = psb("s_marky", [P, 4 * NK], F32)
            s_maskx = psb("s_maskx", [P, 4 * NK], F32)
            s_masky = psb("s_masky", [P, 4 * NK], F32)
            s_ex = psb("s_ex", [P, 4 * NK], F32)
            s_ey = psb("s_ey", [P, 4 * NK], F32)
            s_eexs = psb("s_eexs", [P, 4 * NK], BF16)
            s_eeys = psb("s_eeys", [P, 4 * NK], BF16)
            s_exs = psb("s_exs", [P, 32], F32)
            s_eys = psb("s_eys", [P, 32], F32)
            s_exsb = psb("s_exsb", [P, 32], BF16)
            s_eysb = psb("s_eysb", [P, 32], BF16)
            s_allE = psb("s_allE", [P, 4 * NK], BF16)
            s_top8all = psb("s_top8all", [P, 32 * NT], F32)
            s_Z = psb("s_Z", [P, 4], F32)

            for i in range(8):
                b.dma("gpsimd", s_wkT[:, D * i:D * (i + 1)], wkT.rearrange("(i p) e -> i p e", p=P)[i], "ldg")
                b.dma("gpsimd", s_wvT[:, D * i:D * (i + 1)], wvT.rearrange("(i p) e -> i p e", p=P)[i], "ldg")
                b.dma("gpsimd", s_wqT[:, D * i:D * (i + 1)], wqT.rearrange("(i p) e -> i p e", p=P)[i], "ldg")
                b.dma("gpsimd", s_wqpk[:, 512 * i:512 * (i + 1)], WqT_pk.rearrange("(i p) e -> i p e", p=P)[i], "ldg")
            b.dma("gpsimd", s_keysT[:], keysT, "ldg")
            b.dma("sync", s_bkB[:], bkB, "ld")
            b.dma("sync", s_bvB[:], bvB, "ld")
            b.dma("sync", s_bqc[:], bq_col, "ld")
            b.dma("sync", s_bkc[:], bk_col, "ld")
            for eng in ("tensor", "vector", "scalar"):
                b.wait(eng, "ldg")
                b.wait(eng, "ld")
            b.wait("gpsimd", "ldg")

            def ln_tile(buf):
                VE.reduce_sum(s_mu[buf][:], s_x[buf][:], axis=AX.X)
                VE.tensor_scalar_mul(s_nmu[buf][:], s_mu[buf][:], -1.0 / D)
                VE.tensor_scalar_mul(s_mu[buf][:], s_mu[buf][:], 1.0 / D, inc="v_mu")
                b.wait("scalar", "v_mu")
                SC.activation(s_sq[:], s_x[buf][:], func=AF.Square, bias=s_nmu[buf][:], scale=1.0,
                              accum_out=s_var[buf][:], inc="sc_var")
                b.wait("vector", "sc_var")
                VE.tensor_scalar(s_var[buf][:], s_var[buf][:], 1.0 / D, 1e-5,
                                 op0=ALU.mult, op1=ALU.add, inc="v_var")
                b.wait("scalar", "v_var")
                SC.activation(s_rstd[buf][:], s_var[buf][:], func=AF.Ln)
                SC.activation(s_rstd[buf][:], s_rstd[buf][:], func=AF.Exp, scale=-0.5, inc="sc_rstd")
                b.wait("vector", "sc_rstd")
                VE.tensor_scalar(s_xnbf[buf][:], s_x[buf][:], s_mu[buf][:], s_rstd[buf][:],
                                 op0=ALU.subtract, op1=ALU.mult, inc="v_xn")

            def transpose_tile(buf, gbuf, ti):
                b.wait("tensor", "v_xn")
                if True:
                    b.wait("tensor", "v_xnT_evac")
                for i in range(8):
                    TE.transpose(pbt[:, P * i:P * (i + 1)], s_xnbf[buf][:, P * i:P * (i + 1)], c_ident[:],
                                 inc="t_tp" if i == 7 else None)
                b.wait("vector", "t_tp")
                dst = s_xnTg[gbuf][:].rearrange("p (k t) -> p k t", k=8)[:, :, P * ti:P * (ti + 1)]
                VE.tensor_copy(dst, pbt[:].rearrange("p (k t) -> p k t", k=8), inc="v_xnT_evac")

            def kv_tile(gbuf, ti, kvbuf, do_spill, g):
                b.wait("tensor", "v_kv")
                if do_spill:
                    b.wait("vector", f"sp_kv{kvbuf}")
                for which, wT, bB, dst, scr in ((0, s_wkT, s_bkB, s_K, scr_Kn), (1, s_wvT, s_bvB, s_V, scr_V)):
                    bank = pbk[2 * which:2 * which + 2]
                    for n in range(2):
                        for k in range(8):
                            TE.matmul(bank[n][:],
                                      s_xnTg[gbuf][:, G * k + P * ti: G * k + P * (ti + 1)],
                                      wT[:, D * k + 512 * n: D * k + 512 * (n + 1)],
                                      start=(k == 0), stop=(k == 7),
                                      inc="t_kv" if (k == 7) else None)
                    b.wait("vector", "t_kv")
                    for n in range(2):
                        VE.tensor_tensor(dst[kvbuf][:, 512 * n:512 * (n + 1)], bank[n][:],
                                         bB[:, 512 * n:512 * (n + 1)], op=ALU.add,
                                         inc="v_kv" if n == 1 else None)
                    if do_spill:
                        b.wait("sync", "v_kv")
                        b.dma("sync", scr[g, P * ti:P * (ti + 1), :], dst[kvbuf][:], f"sp_kv{kvbuf}")

            for g in range(NG if max_pass >= 0 else 0):
                for ti in range(TPG):
                    t_abs = g * TPG + ti
                    buf = t_abs % 2
                    gbuf = g % 2
                    if b.cnt.get("v_xn", 0) >= 2:
                        b.wait("sync", "v_xn", b.cnt["v_xn"] - 1)
                    ldx = b.dma("sync", s_x[buf][:], x_pre.rearrange("(n p) d -> n p d", p=P)[t_abs], f"ldx{buf}")
                    b.wait("vector", f"ldx{buf}", ldx)
                    b.wait("scalar", f"ldx{buf}", ldx)
                    ln_tile(buf)
                    transpose_tile(buf, gbuf, ti)
                    b.wait("tensor", "v_xnT_evac")
                    kv_tile(gbuf, ti, buf, False, g)
                    b.wait("tensor", "v_kv")
                    for h in range(NH):
                        TE.matmul(pbk[5 + h // 8][0:64, 64 * (h % 8):64 * (h % 8) + 64],
                                  s_K[buf][:, DH * h:DH * (h + 1)],
                                  s_V[buf][:, DH * h:DH * (h + 1)],
                                  start=(t_abs == 0 and h % 8 == 0),
                                  stop=(t_abs == NT - 1 and h % 8 == 7),
                                  skip_group_check=True,
                                  inc="t_state" if h == NH - 1 else None)
            if max_pass >= 0:
                b.wait("vector", "t_state")
                VE.tensor_scalar(s_S[0:64, 0:512], pbk[5][0:64, :], c_presc[0:64, :], None,
                                 op0=ALU.mult)
                VE.tensor_scalar(s_S[0:64, 512:1024], pbk[6][0:64, :], c_presc[0:64, :], None,
                                 op0=ALU.mult, inc="v_sready")
            if debug and b.cnt.get("v_sready"):
                b.wait("sync", "v_sready")
                b.dma("sync", scr_S, s_S[0:64, :], "dbg")

            b.barrier("p0")

            for g in range(NG if max_pass >= 1 else 0):
                gbuf = g % 2
                b.wait("vector", f"sp_xnT{gbuf}")
                b.wait("vector", "sp_qk")
                for ti in range(TPG):
                    t_abs = g * TPG + ti
                    buf = t_abs % 2
                    if b.cnt.get("v_xn", 0) >= 2:
                        b.wait("sync", "v_xn", b.cnt["v_xn"] - 1)
                    ldx = b.dma("sync", s_x[buf][:], x_own.rearrange("(n p) d -> n p d", p=P)[t_abs], f"ldx{buf}")
                    b.wait("vector", f"ldx{buf}", ldx)
                    b.wait("scalar", f"ldx{buf}", ldx)
                    ln_tile(buf)
                    transpose_tile(buf, gbuf, ti)
                    b.wait("tensor", "v_xnT_evac")
                    kv_tile(gbuf, ti, buf, True, g)
                if p1_stage < 1:
                    continue
                b.wait("sync", "v_xnT_evac")
                b.dma("sync", scr_xnT[g].rearrange("(k p) t -> p k t", p=P),
                      s_xnTg[gbuf][:].rearrange("p (k t) -> p k t", k=8), f"sp_xnT{gbuf}")
                if p1_stage < 2:
                    continue
                for which, wT, bcol, dstT, scr in ((0, s_wqT, s_bqc, s_QTg, scr_QT), (1, s_wkT, s_bkc, s_KTg, scr_KT)):
                    for hp in range(8):
                        bank = pbk[hp % 2]
                        for k in range(8):
                            TE.matmul(bank[:],
                                      wT[:, D * k + P * hp: D * k + P * (hp + 1)],
                                      s_xnTg[gbuf][:, G * k:G * (k + 1)],
                                      start=(k == 0), stop=(k == 7),
                                      inc="t_qk" if k == 7 else None)
                        b.wait("vector", "t_qk")
                        VE.tensor_scalar(dstT[:, G * hp:G * (hp + 1)], bank[:],
                                         bcol[:, hp:hp + 1], None,
                                         op0=ALU.add, inc="v_qk")
                        b.wait("tensor", "v_qk", b.cnt["v_qk"] - 1)
                    b.wait("sync", "v_qk")
                    b.dma("sync", scr[g], dstT[:], "sp_qk")
                if p1_stage < 3:
                    continue
                b.wait("scalar", "t_sim")
                for j in range(8):
                    bank = pbk[2 + (j % 2)]
                    sub = j % 2 * 0
                    for k in range(8):
                        TE.matmul(bank[0:64, :],
                                  s_wqpk[:, 512 * k + 64 * j: 512 * k + 64 * (j + 1)],
                                  s_xnTg[gbuf][:, G * k:G * (k + 1)],
                                  start=(k == 0), stop=(k == 7),
                                  skip_group_check=True,
                                  inc="t_qs" if k == 7 else None)
                    b.wait("scalar", "t_qs")
                    SC.copy(s_qsT[0:64, G * j:G * (j + 1)], bank[0:64, :], inc="sc_qs")
                    b.wait("tensor", "sc_qs")
                for ti in range(TPG if p1_stage >= 4 else 0):
                    t_abs = g * TPG + ti
                    b.wait("tensor", "v_simevac")
                    for p_ in range(2):
                        for h in range(4):
                            jj = p_ * 4 + h
                            TE.matmul(pbk[4][:, (p_ * 4 + h) * 64:(p_ * 4 + h) * 64 + 64],
                                      s_qsT[0:64, G * jj + P * ti:G * jj + P * (ti + 1)],
                                      s_keysT[0:64, (p_ * 4 + h) * 64:(p_ * 4 + h) * 64 + 64],
                                      start=True, stop=True, skip_group_check=True,
                                      inc="t_sim" if (p_ == 1 and h == 3) else None)
                    b.wait("scalar", "t_sim")
                    b.wait("scalar", "v_mask")
                    b.wait("scalar", "sc_exp")
                    b.wait("scalar", "sp_ee")
                    SC.copy(s_simall[:], pbk[4][:], inc="v_simevac")
                    b.wait("vector", "v_simevac")
                    NH4 = range(4 if p1_stage >= 5 else 0)
                    rounds = 2 if t_abs == 0 else 1
                    for rnd in range(rounds):
                        if rnd == 1:
                            b.wait("vector", "sc_exp")
                        xs = lambda h: slice(h * 64, h * 64 + 64)
                        ys = lambda h: slice(256 + h * 64, 256 + h * 64 + 64)
                        for h in NH4:
                            VE.max(out=s_sxA[:, 8 * h:8 * h + 8], in_=s_simall[:, xs(h)])
                            VE.max(out=s_syA[:, 8 * h:8 * h + 8], in_=s_simall[:, ys(h)],
                                   inc="v_max" if h == 3 else None)
                        b.wait("vector", "v_max")
                        for h in NH4:
                            VE.tensor_tensor(s_mA[:, h:h + 1], s_sxA[:, 8 * h:8 * h + 1],
                                             s_syA[:, 8 * h:8 * h + 1], op=ALU.add)
                        for h in NH4:
                            VE.tensor_scalar_mul(s_nmA[:, h:h + 1], s_mA[:, h:h + 1], -1.0,
                                                 inc="v_m" if h == 3 else None)
                        for h in NH4:
                            VE.tensor_single_scalar(s_maskx[:, 64 * h:64 * h + 64], s_simall[:, xs(h)],
                                                    s_sxA[:, 8 * h + 7:8 * h + 8], op=ALU.is_ge)
                            VE.tensor_single_scalar(s_masky[:, 64 * h:64 * h + 64], s_simall[:, ys(h)],
                                                    s_syA[:, 8 * h + 7:8 * h + 8], op=ALU.is_ge,
                                                    inc="v_mask" if h == 3 else None)
                        b.wait("scalar", "v_m")
                        for h in NH4:
                            SC.activation(s_ex[:, 64 * h:64 * h + 64], s_simall[:, xs(h)], func=AF.Exp,
                                          bias=s_nmA[:, h:h + 1], scale=1.0)
                            SC.activation(s_ey[:, 64 * h:64 * h + 64], s_simall[:, ys(h)], func=AF.Exp,
                                          scale=1.0)
                            SC.activation(s_exs[:, 8 * h:8 * h + 8], s_sxA[:, 8 * h:8 * h + 8], func=AF.Exp,
                                          bias=s_nmA[:, h:h + 1], scale=1.0)
                            SC.activation(s_eys[:, 8 * h:8 * h + 8], s_syA[:, 8 * h:8 * h + 8], func=AF.Exp,
                                          scale=1.0, inc="sc_exp" if h == 3 else None)
                        b.wait("vector", "sc_exp")
                        b.wait("vector", "v_mask")
                        for h in NH4:
                            VE.tensor_mul(s_eexs[:, 64 * h:64 * h + 64], s_ex[:, 64 * h:64 * h + 64],
                                          s_maskx[:, 64 * h:64 * h + 64])
                            VE.tensor_mul(s_eeys[:, 64 * h:64 * h + 64], s_ey[:, 64 * h:64 * h + 64],
                                          s_masky[:, 64 * h:64 * h + 64], inc="v_ee" if h == 3 else None)
                        VE.tensor_copy(s_exsb[:], s_exs[:])
                        VE.tensor_copy(s_eysb[:], s_eys[:])
                        for h in NH4:
                            VE.tensor_tensor(
                                s_allE[:, 64 * h:64 * h + 64].rearrange("p (a c) -> p a c", a=8),
                                s_exsb[:, 8 * h:8 * h + 8].rearrange("p (a c) -> p a c", c=1).to_broadcast([P, 8, 8]),
                                s_eysb[:, 8 * h:8 * h + 8].rearrange("p (a c) -> p a c", a=1).to_broadcast([P, 8, 8]),
                                op=ALU.mult, inc="v_alle" if h == 3 else None)
                        b.wait("vector", "v_alle")
                        for h in NH4:
                            VE.max(out=s_top8all[:, 32 * t_abs + 8 * h:32 * t_abs + 8 * h + 8],
                                   in_=s_allE[:, 64 * h:64 * h + 64],
                                   inc="v_max2" if h == 3 else None)
                    b.wait("sync", "v_ee")
                    b.dma("sync", scr_EEx[t_abs], s_eexs[:], "sp_ee")
                    b.dma("sync", scr_EEy[t_abs], s_eeys[:], "sp_ee")
            if max_pass >= 1 and p1_stage >= 5:
                for _ in range(4):
                    VE.max(out=s_wu8[:], in_=s_wu64[:])
                b.barrier("preT")
                for tp in range(NT):
                    for h in range(4):
                        VE.reduce_sum(s_Z[:, h:h + 1], s_top8all[:, 32 * tp + 8 * h:32 * tp + 8 * h + 8],
                                      axis=AX.X)
                    for h in range(4):
                        VE.tensor_copy(s_T[:, tp * 4 + h:tp * 4 + h + 1],
                                       s_top8all[:, 32 * tp + 8 * h + 7:32 * tp + 8 * h + 8])
                    for h in range(4):
                        VE.reciprocal(s_rZ[:, tp * 4 + h:tp * 4 + h + 1], s_Z[:, h:h + 1],
                                      inc="v_rz" if h == 3 else None)
            if debug and b.cnt.get("v_rz"):
                b.wait("sync", "v_rz")
                b.dma("sync", scr_T, s_T[:], "dbg")
                b.dma("sync", scr_rZ, s_rZ[:], "dbg")

            for snm in ("sp_xnT0", "sp_xnT1", "sp_qk", "sp_kv0", "sp_kv1", "sp_ee"):
                if b.cnt.get(snm):
                    b.wait("sync", snm)
            b.barrier("p1")

        with ExitStack() as ps:
            def psb(name, shape, dtype):
                return ps.enter_context(nc.sbuf_tensor(name, shape, dtype))

            s_WupT = psb("s_WupT", [P, 8 * I_FFN], BF16)
            s_WdnT = psb("s_WdnT", [P, 32 * D], BF16)
            s_xnTg = [psb(f"p2_xnT{i}", [P, 8 * G], BF16) for i in range(2)]
            s_sup = psb("s_sup", [P, 32 * G], BF16)
            s_dense = [psb(f"s_dense{i}", [P, D], F32) for i in range(2)]

            for i in range(8):
                b.dma("gpsimd", s_WupT[:, I_FFN * i:I_FFN * (i + 1)],
                      WupT.rearrange("(i p) e -> i p e", p=P)[i], "ldg")
            for i in range(32):
                b.dma("gpsimd", s_WdnT[:, D * i:D * (i + 1)],
                      WdownT.rearrange("(i p) e -> i p e", p=P)[i], "ldg")
            for eng in ("tensor", "vector", "scalar"):
                b.wait(eng, "ldg")

            for g in range(NG if max_pass >= 2 else 0):
                gbuf = g % 2
                if g >= 2:
                    b.wait("sync", "t_up")
                b.dma("sync", s_xnTg[gbuf][:].rearrange("p (k t) -> p k t", k=8),
                      scr_xnT[g].rearrange("(k p) t -> p k t", p=P), f"ld_xnT{gbuf}")
                b.wait("tensor", f"ld_xnT{gbuf}")
                b.wait("scalar", "t_dn")
                for m in range(32):
                    bank = pbk[m % 2]
                    if m >= 2:
                        b.wait("tensor", "sc_sup", b.cnt["sc_sup"] - 1)
                    for k in range(8):
                        TE.matmul(bank[:],
                                  s_WupT[:, I_FFN * k + P * m: I_FFN * k + P * (m + 1)],
                                  s_xnTg[gbuf][:, G * k:G * (k + 1)],
                                  start=(k == 0), stop=(k == 7),
                                  inc="t_up" if k == 7 else None)
                    b.wait("scalar", "t_up")
                    SC.activation(s_sup[:, G * m:G * (m + 1)], bank[:], func=AF.Silu, inc="sc_sup")
                b.wait("tensor", "sc_sup")
                for ti in range(TPG):
                    t_abs = g * TPG + ti
                    dbuf = ti % 2
                    banks = pbk[2 + 2 * dbuf: 4 + 2 * dbuf]
                    if b.cnt.get("v_dense", 0) >= 2:
                        b.wait("tensor", "v_dense", b.cnt["v_dense"] - 1)
                    b.wait("vector", f"sp_acc{dbuf}")
                    for m in range(32):
                        for n in range(2):
                            TE.matmul(banks[n][:],
                                      s_sup[:, G * m + P * ti: G * m + P * (ti + 1)],
                                      s_WdnT[:, D * m + 512 * n: D * m + 512 * (n + 1)],
                                      start=(m == 0), stop=(m == 31),
                                      inc="t_dn" if (m == 31 and n == 1) else None)
                    b.wait("vector", "t_dn")
                    for n in range(2):
                        VE.tensor_tensor(s_dense[dbuf][:, 512 * n:512 * (n + 1)], banks[n][:],
                                         s_woB[:, 512 * n:512 * (n + 1)], op=ALU.add,
                                         inc="v_dense" if n == 1 else None)
                    b.wait("sync", "v_dense")
                    b.dma("sync", scr_acc.rearrange("(n p) d -> n p d", p=P)[t_abs], s_dense[dbuf][:], f"sp_acc{dbuf}")

            if b.cnt.get("sp_acc0"):
                b.wait("sync", "sp_acc0")
            if b.cnt.get("sp_acc1"):
                b.wait("sync", "sp_acc1")
            b.barrier("p2")

        with ExitStack() as ps:
            def psb(name, shape, dtype):
                return ps.enter_context(nc.sbuf_tensor(name, shape, dtype))

            HE = NE // 2
            s_deT = psb("s_deT", [P, 8 * HE], BF16)
            s_ue = psb("s_ue", [P, 16 * D], BF16)
            s_xnTg = [psb(f"p3_xnT{i}", [P, 8 * G], BF16) for i in range(2)]
            s_eex = [psb(f"p3_eex{i}", [P, 4 * NK], BF16) for i in range(2)]
            s_eey = [psb(f"p3_eey{i}", [P, 4 * NK], BF16) for i in range(2)]
            s_L = [psb(f"s_L{i}", [P, HE], BF16) for i in range(2)]
            s_E = [psb(f"s_E{i}", [P, HE], BF16) for i in range(2)]
            s_m01 = psb("s_m01", [P, HE], BF16)
            s_Pm = psb("s_Pm", [P, HE], BF16)
            s_q = psb("s_q", [P, HE], BF16)
            s_Wa = [psb(f"s_Wa{i}", [P, HE], BF16) for i in range(2)]
            s_Wtmp = psb("s_Wtmp", [P, HE], BF16)
            s_Wt = psb("s_Wt", [P, HE], BF16)
            s_es = [psb(f"s_es{i}", [P, D], F32) for i in range(2)]

            stashA = [None]
            stashB = [None]

            def flush_A():
                fn = stashA[0]
                stashA[0] = None
                if fn:
                    fn()

            def flush_B():
                fn = stashB[0]
                stashB[0] = None
                if fn:
                    fn()

            def flush_stash():
                flush_A()
                flush_B()

            job = [0]
            for he in range(2 if max_pass >= 3 else 0):
                flush_stash()
                b.wait("gpsimd", "sp_es0")
                b.wait("gpsimd", "sp_es1")
                if he == 1:
                    b.wait("gpsimd", "t_es")
                for i in range(8):
                    b.dma("gpsimd", s_deT[:, HE * i:HE * (i + 1)],
                          deT.rearrange("(i p) e -> i p e", p=P)[i, :, HE * he:HE * (he + 1)], "ldg")
                for i in range(16):
                    b.dma("gpsimd", s_ue[:, D * i:D * (i + 1)],
                          ue_n.rearrange("(i p) e -> i p e", p=P)[16 * he + i], "ldg")
                for eng in ("tensor", "vector", "scalar", "gpsimd"):
                    b.wait(eng, "ldg")
                for g in range(NG):
                    gbuf = g % 2
                    b.wait("sync", "t_L")
                    b.dma("sync", s_xnTg[gbuf][:].rearrange("p (k t) -> p k t", k=8),
                          scr_xnT[g].rearrange("(k p) t -> p k t", p=P), f"ld_xnT{gbuf}")
                    b.wait("tensor", f"ld_xnT{gbuf}")
                    for ti in range(TPG):
                        t_abs = g * TPG + ti
                        j = job[0]
                        p2 = j % 2
                        if b.cnt.get("g_E", 0) >= 8:
                            b.wait("sync", "g_E", b.cnt["g_E"] - 4)
                        ldee = b.dma("sync", s_eex[p2][:], scr_EEx[t_abs], "ld_ee")
                        ldee = b.dma("sync", s_eey[p2][:], scr_EEy[t_abs], "ld_ee")
                        if b.cnt.get("v_L", 0):
                            b.wait("tensor", "v_L")
                        lbanks = (pbk[0], pbk[1], pbk[5], pbk[6])
                        for k in range(8):
                            for c in range(4):
                                TE.matmul(lbanks[c][:],
                                          s_xnTg[gbuf][:, G * k + P * ti: G * k + P * (ti + 1)],
                                          s_deT[:, HE * k + 512 * c: HE * k + 512 * (c + 1)],
                                          start=(k == 0), stop=(k == 7),
                                          inc="t_L" if (k == 7 and c == 3) else None)
                        b.wait("vector", "t_L")
                        for c in range(4):
                            VE.tensor_copy(s_L[p2][:, 512 * c:512 * (c + 1)], lbanks[c][:],
                                           inc="v_L")
                        flush_A()
                        b.wait("gpsimd", "ld_ee", ldee)
                        for h in range(4):
                            tcol = slice(t_abs * 4 + h, t_abs * 4 + h + 1)
                            eb = h % 2
                            exsl = s_eex[p2][:].rearrange("p (h n) -> p h n", h=4)[:, h, 32 * he:32 * he + 32]
                            eysl = s_eey[p2][:].rearrange("p (h n) -> p h n", h=4)[:, h]
                            if b.cnt.get("v_Pm", 0) >= 2:
                                b.wait("gpsimd", "v_Pm", b.cnt["v_Pm"] - 1)
                            GP.tensor_tensor(
                                s_E[eb][:].rearrange("p (a c) -> p a c", a=32),
                                exsl.rearrange("p (a c) -> p a c", c=1).to_broadcast([P, 32, NK]),
                                eysl.rearrange("p (a c) -> p a c", a=1).to_broadcast([P, 32, NK]),
                                op=ALU.mult, inc="g_E")
                            b.wait("vector", "g_E")
                            VE.tensor_single_scalar(s_m01[:], s_E[eb][:], s_T[:, tcol], op=ALU.is_ge)
                            VE.tensor_mul(s_Pm[:], s_m01[:], s_E[eb][:], inc="v_Pm")
                            VE.tensor_scalar_mul(s_q[:], s_Pm[:], s_rZ[:, tcol])
                            VE.tensor_mul(s_q[:], s_q[:], s_L[p2][:], inc="v_q")
                            b.wait("scalar", "v_q")
                            if h == 0:
                                SC.activation(s_Wa[p2][:], s_q[:], func=AF.Silu, inc="sc_W")
                            else:
                                SC.activation(s_Wtmp[:], s_q[:], func=AF.Silu, inc="sc_W")
                                b.wait("vector", "sc_W")
                                VE.tensor_add(s_Wa[p2][:], s_Wa[p2][:], s_Wtmp[:], inc="v_Wacc")

                        v_scW = b.cnt["sc_W"]
                        v_Wacc = b.cnt["v_Wacc"]
                        v_tes_prev = b.cnt.get("t_es", 0)

                        def mk(j=j, p2=p2, t_abs=t_abs, v_scW=v_scW, v_Wacc=v_Wacc,
                               v_tes_prev=v_tes_prev):
                            def go():
                                ep = j % 2
                                b.wait("tensor", "sc_W", v_scW)
                                b.wait("tensor", "v_Wacc", v_Wacc)
                                b.wait("scalar", "t_es", v_tes_prev)
                                for r in range(2):
                                    b.wait("tensor", "sc_Wt")
                                    for i in range(8):
                                        TE.transpose(pbt[:, P * i:P * (i + 1)],
                                                     s_Wa[p2][:, P * (8 * r + i):P * (8 * r + i + 1)],
                                                     c_ident[:],
                                                     inc="t_Wt" if i == 7 else None)
                                    b.wait("scalar", "t_Wt")
                                    SC.copy(s_Wt[:, P * 8 * r:P * 8 * (r + 1)], pbt[:], inc="sc_Wt")
                                b.wait("tensor", "sc_Wt")
                                banks = pbk[2:4]
                                if j >= 1:
                                    b.wait("tensor", "v_es", j)
                                for m in range(16):
                                    for n in range(2):
                                        TE.matmul(banks[n][:],
                                                  s_Wt[:, P * m:P * (m + 1)],
                                                  s_ue[:, D * m + 512 * n: D * m + 512 * (n + 1)],
                                                  start=(m == 0), stop=(m == 15),
                                                  inc="t_es" if (m == 15 and n == 1) else None)
                            return go

                        def mkB(j=j, t_abs=t_abs):
                            def go():
                                ep = j % 2
                                banks = pbk[2:4]
                                b.wait("vector", "t_es")
                                b.wait("vector", f"sp_es{ep}")
                                for n in range(2):
                                    VE.tensor_copy(s_es[ep][:, 512 * n:512 * (n + 1)], banks[n][:],
                                                   inc="v_es" if n == 1 else None)
                                b.wait("gpsimd", "v_es")
                                b.dma("gpsimd", scr_acc.rearrange("(n p) d -> n p d", p=P)[t_abs],
                                      s_es[ep][:], f"sp_es{ep}", accum_op=ALU.add)
                            return go

                        flush_B()
                        stashA[0] = mk()
                        stashB[0] = mkB()
                        job[0] += 1
            flush_stash()

            if b.cnt.get("sp_es0"):
                b.wait("gpsimd", "sp_es0")
            if b.cnt.get("sp_es1"):
                b.wait("gpsimd", "sp_es1")
            b.barrier("p3")

        with ExitStack() as ps:
            def psb(name, shape, dtype):
                return ps.enter_context(nc.sbuf_tensor(name, shape, dtype))

            s_woT = psb("s_woT", [P, 8 * D], BF16)
            s_QT = [psb(f"s_QT{i}", [64, NH * G], BF16) for i in range(2)]
            s_KT = [psb(f"s_KT{i}", [64, NH * G], BF16) for i in range(2)]
            s_Kn = [psb(f"s_Kn{i}", [P, 4 * D], BF16) for i in range(2)]
            s_Vn = [psb(f"s_Vn{i}", [P, 4 * D], BF16) for i in range(2)]
            s_sT = [psb(f"s_sT{i}", [P, 16 * P], BF16) for i in range(4)]
            s_AT = psb("s_AT", [P, 8 * G], BF16)
            s_xf = [psb(f"s_xf{i}", [P, D], F32) for i in range(2)]
            s_af = [psb(f"s_af{i}", [P, D], F32) for i in range(2)]
            s_out = [psb(f"s_out{i}", [P, D], F32) for i in range(2)]

            for i in range(8):
                b.dma("gpsimd", s_woT[:, D * i:D * (i + 1)], woT.rearrange("(i p) e -> i p e", p=P)[i], "ldg")
            for eng in ("tensor", "vector"):
                b.wait(eng, "ldg")
            VE.tensor_copy(s_Sbf[0:64, :], s_S[0:64, :], inc="v_sbf")
            b.wait("tensor", "v_sbf")

            for g in range(NG if max_pass >= 4 else 0):
                gp = g % 2
                if g == 0:
                    b.dma("sync", s_QT[gp][0:64, :].rearrange("p (hp s c) -> p hp s c", hp=8, s=2),
                          scr_QT[g].rearrange("(s p) (hp c) -> p hp s c", s=2, hp=8), f"ld_g4{gp}")
                    b.dma("sync", s_KT[gp][0:64, :].rearrange("p (hp s c) -> p hp s c", hp=8, s=2),
                          scr_KT[g].rearrange("(s p) (hp c) -> p hp s c", s=2, hp=8), f"ld_g4{gp}")
                    b.dma("sync", s_Kn[gp][:].rearrange("p (c d) -> p c d", c=4),
                          scr_Kn[g].rearrange("(c p) d -> p c d", p=P), f"ld_g4{gp}")
                    b.dma("sync", s_Vn[gp][:].rearrange("p (c d) -> p c d", c=4),
                          scr_V[g].rearrange("(c p) d -> p c d", p=P), f"ld_g4{gp}")
                if g + 1 < NG:
                    b.wait("sync", "t_st")
                    b.dma("sync", s_QT[gp ^ 1][0:64, :].rearrange("p (hp s c) -> p hp s c", hp=8, s=2),
                          scr_QT[g + 1].rearrange("(s p) (hp c) -> p hp s c", s=2, hp=8), f"ld_g4{gp ^ 1}")
                    b.dma("sync", s_KT[gp ^ 1][0:64, :].rearrange("p (hp s c) -> p hp s c", hp=8, s=2),
                          scr_KT[g + 1].rearrange("(s p) (hp c) -> p hp s c", s=2, hp=8), f"ld_g4{gp ^ 1}")
                    b.dma("sync", s_Kn[gp ^ 1][:].rearrange("p (c d) -> p c d", c=4),
                          scr_Kn[g + 1].rearrange("(c p) d -> p c d", p=P), f"ld_g4{gp ^ 1}")
                    b.dma("sync", s_Vn[gp ^ 1][:].rearrange("p (c d) -> p c d", c=4),
                          scr_V[g + 1].rearrange("(c p) d -> p c d", p=P), f"ld_g4{gp ^ 1}")
                b.wait("tensor", f"ld_g4{gp}")
                b.wait("tensor", "v_out")
                for ti in range(TPG):
                    b.wait("tensor", "v_sT")
                    for h in range(NH):
                        TE.matmul(pbk[h // 4][:, P * (h % 4):P * (h % 4 + 1)],
                                  s_KT[gp][0:64, G * h + P * ti: G * h + P * (ti + 1)],
                                  s_QT[gp][0:64, G * h + P * ti: G * h + P * (ti + 1)],
                                  start=True, stop=True, skip_group_check=True,
                                  inc="t_sc" if h == NH - 1 else None)
                    b.wait("vector", "t_sc")
                    for bk in range(4):
                        VE.tensor_tensor(
                            s_sT[ti][:].rearrange("p (h t) -> p h t", h=16)[:, 4 * bk:4 * bk + 4],
                            pbk[bk][:].rearrange("p (h t) -> p h t", h=4),
                            c_triu[:].rearrange("p (o t) -> p o t", o=1).to_broadcast([P, 4, P]),
                            op=ALU.mult, inc="v_sT" if bk == 3 else None)
                b.wait("tensor", "v_sT")
                for ti in range(TPG):
                    t_abs = g * TPG + ti
                    b.wait("tensor", "v_AT")
                    b.wait("tensor", "v_sbf")
                    for h in range(NH):
                        hp = h // 2
                        bank = pbk[hp // 4]
                        reg = bank[64 * (h % 2):64 * (h % 2) + 64, P * (hp % 4):P * (hp % 4) + P]
                        TE.matmul(reg, s_Vn[gp][:, D * ti + DH * h: D * ti + DH * (h + 1)],
                                  s_sT[ti][:, P * h:P * (h + 1)],
                                  start=True, stop=False, tile_position=(0, 64 * (h % 2)),
                                  skip_group_check=True)
                        TE.matmul(reg, s_Sbf[0:64, DH * h:DH * (h + 1)],
                                  s_QT[gp][0:64, G * h + P * ti: G * h + P * (ti + 1)],
                                  start=False, stop=True, tile_position=(0, 64 * (h % 2)),
                                  skip_group_check=True,
                                  inc="t_AT" if h == NH - 1 else None)
                    for h in range(NH):
                        TE.matmul(pbk[5 + h // 8][0:64, 64 * (h % 8):64 * (h % 8) + 64],
                                  s_Kn[gp][:, D * ti + DH * h: D * ti + DH * (h + 1)],
                                  s_Vn[gp][:, D * ti + DH * h: D * ti + DH * (h + 1)],
                                  start=True, stop=True,
                                  skip_group_check=True,
                                  inc="t_st" if h == NH - 1 else None)
                    b.wait("vector", "t_AT")
                    for bk in range(2):
                        VE.tensor_copy(
                            s_AT[:].rearrange("p (i t) -> p i t", i=8)[:, 4 * bk:4 * bk + 4, P * ti:P * (ti + 1)],
                            pbk[bk][:].rearrange("p (r t) -> p r t", r=4),
                            inc="v_AT" if bk == 1 else None)
                    b.wait("vector", "t_st")
                    VE.tensor_add(s_S[0:64, 0:512], s_S[0:64, 0:512], pbk[5][0:64, :])
                    VE.tensor_add(s_S[0:64, 512:1024], s_S[0:64, 512:1024], pbk[6][0:64, :])
                    VE.tensor_copy(s_Sbf[0:64, :], s_S[0:64, :], inc="v_sbf")
                b.wait("tensor", "v_AT")
                for ti in range(TPG):
                    t_abs = g * TPG + ti
                    obuf = ti % 2
                    banks = pbk[2 * obuf:2 * obuf + 2]
                    if b.cnt.get("v_out", 0) >= 2:
                        b.wait("sync", "v_out", b.cnt["v_out"] - 1)
                    ldf = b.dma("sync", s_xf[obuf][:], x_own.rearrange("(n p) d -> n p d", p=P)[t_abs], f"ld_fin{obuf}")
                    ldf = b.dma("sync", s_af[obuf][:], scr_acc.rearrange("(n p) d -> n p d", p=P)[t_abs], f"ld_fin{obuf}")
                    if b.cnt.get("v_out", 0) >= 2:
                        b.wait("tensor", "v_out", b.cnt["v_out"] - 1)
                    for i in range(8):
                        for n in range(2):
                            TE.matmul(banks[n][:],
                                      s_AT[:, G * i + P * ti: G * i + P * (ti + 1)],
                                      s_woT[:, D * i + 512 * n: D * i + 512 * (n + 1)],
                                      start=(i == 0), stop=(i == 7),
                                      inc="t_O" if (i == 7 and n == 1) else None)
                    b.wait("vector", "t_O")
                    b.wait("vector", f"ld_fin{obuf}", ldf)
                    b.wait("vector", f"st_out{obuf}")
                    for n in range(2):
                        VE.tensor_tensor(s_out[obuf][:, 512 * n:512 * (n + 1)], banks[n][:],
                                         s_af[obuf][:, 512 * n:512 * (n + 1)], op=ALU.add)
                    VE.tensor_add(s_out[obuf][:], s_out[obuf][:], s_xf[obuf][:], inc="v_out")
                    b.wait("sync", "v_out")
                    b.dma("sync", out_d.rearrange("(n p) d -> n p d", p=P)[t_abs], s_out[obuf][:], f"st_out{obuf}")

            if b.cnt.get("st_out0"):
                b.wait("sync", "st_out0")
                b.wait("sync", "st_out1")

        b.run(block)

    return nc


def _host_inputs(inputs):
    f32 = np.float32
    x = np.asarray(inputs["L0_attn_x"], f32)
    wq = np.asarray(inputs["L0_attn_wq_w"], f32); bq = np.asarray(inputs["L0_attn_wq_b"], f32)
    wk = np.asarray(inputs["L0_attn_wk_w"], f32); bk = np.asarray(inputs["L0_attn_wk_b"], f32)
    wv = np.asarray(inputs["L0_attn_wv_w"], f32); bv = np.asarray(inputs["L0_attn_wv_b"], f32)
    wo = np.asarray(inputs["L0_attn_wo_w"], f32); bo = np.asarray(inputs["L0_attn_wo_b"], f32)
    Wup = np.asarray(inputs["L0_ffn_W_up"], f32)
    Wdown = np.asarray(inputs["L0_ffn_W_down"], f32)
    Wq_pk = np.asarray(inputs["L0_ffn_W_q"], f32)
    keys = np.asarray(inputs["L0_ffn_keys"], f32)
    de = np.asarray(inputs["L0_ffn_down_embed"], f32)
    ue = np.asarray(inputs["L0_ffn_up_embed"], f32)

    common = {
        "wqT": np.ascontiguousarray(wq.T), "wkT": np.ascontiguousarray(wk.T),
        "wvT": np.ascontiguousarray(wv.T), "woT": np.ascontiguousarray(wo.T),
        "bq_col": np.ascontiguousarray(bq.reshape(8, P).T), "bk_col": np.ascontiguousarray(bk.reshape(8, P).T),
        "bkB": np.ascontiguousarray(np.broadcast_to(bk, (P, D))),
        "bvB": np.ascontiguousarray(np.broadcast_to(bv, (P, D))),
        "woB": np.ascontiguousarray(np.broadcast_to(bo, (P, D))),
        "WupT": np.ascontiguousarray(Wup.T), "WdownT": np.ascontiguousarray(Wdown.T),
        "WqT_pk": np.ascontiguousarray(Wq_pk.T),
        "keysT": np.ascontiguousarray(keys.transpose(3, 2, 0, 1).reshape(64, 512)),
        "deT": np.ascontiguousarray(de.T), "ue_n": np.ascontiguousarray(ue),
        "ident_bf": np.eye(P, dtype=ml_dtypes.bfloat16),
        "triu_f": np.triu(np.ones((P, P), dtype=f32)),
    }
    in_maps = []
    for c in range(8):
        bi, h = c // 2, c % 2
        m = dict(common)
        m["x_own"] = np.ascontiguousarray(x[bi, TOK * h:TOK * (h + 1)])
        m["x_pre"] = np.ascontiguousarray(x[bi, 0:TOK])
        m["pre_scale"] = np.full((P, 1), 1.0 if h == 1 else 0.0, f32)
        in_maps.append(m)
    return in_maps


def kernel(**inputs):
    in_maps = _host_inputs(inputs)
    nc = build_nc()
    res = run_bass_kernel_spmd(nc, in_maps, core_ids=list(range(8)))
    out = np.empty((4, 2 * TOK, D), np.float32)
    for c in range(8):
        bi, h = c // 2, c % 2
        out[bi, TOK * h:TOK * (h + 1)] = np.asarray(res.results[c]["out"], np.float32)
    return out


# version 09 (final)
# speedup vs baseline: 1.0681x; 1.0022x over previous
import numpy as np
import ml_dtypes
import concourse.bass as bass
import concourse.mybir as mybir
from concourse.bass_utils import run_bass_kernel_spmd
from contextlib import ExitStack

F32 = mybir.dt.float32
BF16 = mybir.dt.bfloat16
AF = mybir.ActivationFunctionType
ALU = mybir.AluOpType
AX = mybir.AxisListType

P = 128
D = 1024
NH = 16
DH = 64
I_FFN = 4096
NE = 4096
NK = 64
TOK = 4096
G = 512
NG = TOK // G
TPG = G // P
NT = TOK // P
ENGINES = ("sync", "tensor", "vector", "scalar", "gpsimd")


class EngProxy:

    def __init__(self, b, eng):
        self._b = b
        self._eng = eng

    def __getattr__(self, meth):
        b, eng = self._b, self._eng

        def call(*a, inc=None, incn=1, **kw):
            if inc is None:
                b.q[eng].append(lambda e, m=meth, a=a, kw=kw: getattr(e, m)(*a, **kw))
                return None
            h = b.semh(inc)
            b.cnt[inc] += incn
            b.q[eng].append(lambda e, m=meth, a=a, kw=kw, h=h, n=incn: getattr(e, m)(*a, **kw).then_inc(h, n))
            return b.cnt[inc]

        return call


class Builder:
    def __init__(self, nc, es):
        self.nc = nc
        self.es = es
        self.q = {e: [] for e in ENGINES}
        self.sems = {}
        self.cnt = {}
        self.e = {e: EngProxy(self, e) for e in ENGINES}

    def semh(self, name):
        if name not in self.sems:
            self.sems[name] = self.es.enter_context(self.nc.semaphore(name))
            self.cnt[name] = 0
        return self.sems[name]

    def wait(self, eng, name, val=None):
        h = self.semh(name)
        v = self.cnt[name] if val is None else val
        self.q[eng].append(lambda e, h=h, v=v: e.wait_ge(h, v))

    def dma(self, eng, out, in_, sem_name="dma", **kw):
        h = self.semh(sem_name)
        self.cnt[sem_name] += 16
        self.q[eng].append(lambda e, o=out, i=in_, h=h, kw=dict(kw): e.dma_start(out=o, in_=i, **kw).then_inc(h, 16))
        return self.cnt[sem_name]

    def sem_inc(self, eng, name, n=1):
        h = self.semh(name)
        self.cnt[name] += n
        self.q[eng].append(lambda e, h=h, n=n: e.sem_inc(h, n))
        return self.cnt[name]

    def barrier(self, tag):
        name = "bar"
        h = self.semh(name)
        self.cnt[name] += len(ENGINES)
        v = self.cnt[name]
        for eng in ENGINES:
            self.q[eng].append(lambda e, h=h: e.sem_inc(h, 1))
        for eng in ENGINES:
            self.q[eng].append(lambda e, h=h, v=v: e.wait_ge(h, v))

    def run(self, block):
        qs = self.q

        @block.sync
        def _(e):
            for fn in qs["sync"]:
                fn(e)

        @block.tensor
        def _(e):
            for fn in qs["tensor"]:
                fn(e)

        @block.vector
        def _(e):
            for fn in qs["vector"]:
                fn(e)

        @block.scalar
        def _(e):
            for fn in qs["scalar"]:
                fn(e)

        @block.gpsimd
        def _(e):
            for fn in qs["gpsimd"]:
                fn(e)


def build_nc(debug=False, max_pass=99, p1_stage=99, zscr=False):
    nc = bass.Bass("TRN2", target_bir_lowering=False, detect_race_conditions=False)
    dram = {}

    def din(name, shape, dtype=F32):
        dram[name] = nc.dram_tensor(name, shape, dtype, kind="ExternalInput").ap()
        return dram[name]

    x_own = din("x_own", [TOK, D])
    x_pre = din("x_pre", [TOK, D])
    pre_scale = din("pre_scale", [P, 1])
    wqT = din("wqT", [D, D]); wkT = din("wkT", [D, D]); wvT = din("wvT", [D, D]); woT = din("woT", [D, D])
    bq_col = din("bq_col", [P, 8]); bk_col = din("bk_col", [P, 8])
    bkB = din("bkB", [P, D]); bvB = din("bvB", [P, D]); woB = din("woB", [P, D])
    WupT = din("WupT", [D, I_FFN]); WdownT = din("WdownT", [I_FFN, D])
    WqT_pk = din("WqT_pk", [D, 512])
    keysT = din("keysT", [64, 512])
    deT = din("deT", [D, NE]); ue_n = din("ue_n", [NE, D])
    ident_bf = din("ident_bf", [P, P], BF16)
    triu_f = din("triu_f", [P, P])

    out_d = nc.dram_tensor("out", [TOK, D], F32, kind="ExternalOutput").ap()

    skind = "ExternalOutput" if (debug or zscr) else "Internal"

    def scrt(name, shape, dtype):
        return nc.dram_tensor(name, shape, dtype, kind=skind).ap()

    scr_xnT = scrt("scr_xnT", [NG, D, G], BF16)
    scr_QT = scrt("scr_QT", [NG, P, 8 * G], BF16)
    scr_KT = scrt("scr_KT", [NG, P, 8 * G], BF16)
    scr_Kn = scrt("scr_Kn", [NG, G, D], BF16)
    scr_V = scrt("scr_V", [NG, G, D], BF16)
    scr_EEx = scrt("scr_EEx", [NT, P, 4 * NK], BF16)
    scr_EEy = scrt("scr_EEy", [NT, P, 4 * NK], BF16)
    scr_acc = scrt("scr_acc", [TOK, D], F32)
    if debug or zscr:
        scr_T = scrt("scr_T", [P, NT * 4], F32)
        scr_rZ = scrt("scr_rZ", [P, NT * 4], F32)
        scr_S = scrt("scr_S", [64, 1024], F32)

    es = ExitStack()
    with es:
        b = Builder(nc, es)
        SY, TE, VE, SC, GP = (b.e[k] for k in ENGINES)

        def sb(name, shape, dtype):
            return es.enter_context(nc.sbuf_tensor(name, shape, dtype))

        c_ident = sb("c_ident", [P, P], BF16)
        c_triu = sb("c_triu", [P, P], F32)
        c_presc = sb("c_presc", [P, 1], F32)
        s_T = sb("s_T", [P, NT * 4], F32)
        s_rZ = sb("s_rZ", [P, NT * 4], F32)
        s_S = sb("s_S", [64, 1024], F32)
        s_Sbf = sb("s_Sbf", [64, 1024], BF16)
        s_woB = sb("s_woB", [P, D], F32)

        pbk = [es.enter_context(nc.psum_tensor(f"pb{i}", [P, 512], F32)) for i in range(7)]
        pbt = es.enter_context(nc.psum_tensor("pbt", [P, 1024], BF16))
        s_wu8 = sb("s_wu8", [P, 8], F32)
        s_wu64 = sb("s_wu64", [P, 64], F32)

        block = es.enter_context(nc.Block())

        b.dma("sync", c_ident[:], ident_bf, "ld")
        b.dma("sync", c_triu[:], triu_f, "ld")
        b.dma("sync", c_presc[:], pre_scale, "ld")
        b.dma("sync", s_woB[:], woB, "ld")
        b.wait("vector", "ld")
        b.wait("scalar", "ld")
        for rnd in range(2):
            VE.tensor_scalar_add(s_wu64[:], c_triu[:, 0:64], 1.0)
            VE.reduce_sum(s_wu8[:, 0:1], s_wu64[:], axis=AX.X)
            VE.scalar_tensor_tensor(s_wu64[:], s_wu64[:], 1.0, s_wu64[:],
                                    op0=ALU.mult, op1=ALU.mult, accum_out=s_wu8[:, 1:2])
            VE.max(out=s_wu8[:], in_=s_wu64[:])
            VE.match_replace(out=s_wu64[:], in_to_replace=s_wu8[:], in_values=s_wu64[:], imm_value=7.0)
            VE.reciprocal(s_wu8[:], s_wu8[:])
            VE.tensor_scalar(s_wu64[:], s_wu64[:], s_wu8[:, 0:1], s_wu8[:, 1:2],
                             op0=ALU.mult, op1=ALU.add, inc="wuv")
            b.wait("scalar", "wuv")
            SC.activation(s_wu64[:], s_wu64[:], func=AF.Square, scale=0.01,
                          accum_out=s_wu8[:, 2:3])
            SC.activation(s_wu64[:], s_wu64[:], func=AF.Ln, scale=0.001)
            SC.activation(s_wu64[:], s_wu64[:], func=AF.Exp, scale=0.001)
            SC.activation(s_wu64[:], s_wu64[:], func=AF.Silu, scale=0.1, inc="wus")
            b.wait("vector", "wus")
        b.barrier("warm")

        with ExitStack() as ps:
            def psb(name, shape, dtype):
                return ps.enter_context(nc.sbuf_tensor(name, shape, dtype))

            s_wkT = psb("s_wkT", [P, 8 * D], BF16)
            s_wvT = psb("s_wvT", [P, 8 * D], BF16)
            s_wqT = psb("s_wqT", [P, 8 * D], BF16)
            s_wqpk = psb("s_wqpk", [P, 8 * 512], BF16)
            s_keysT = psb("s_keysT", [64, 512], BF16)
            s_bkB = psb("s_bkB", [P, D], F32)
            s_bvB = psb("s_bvB", [P, D], F32)
            s_bqc = psb("s_bqc", [P, 8], F32)
            s_bkc = psb("s_bkc", [P, 8], F32)
            s_x = [psb(f"s_x{i}", [P, D], F32) for i in range(2)]
            s_xnbf = [psb(f"s_xnbf{i}", [P, D], BF16) for i in range(2)]
            s_xnTg = [psb(f"s_xnTg{i}", [P, 8 * G], BF16) for i in range(2)]
            s_K = [psb(f"s_K{i}", [P, D], BF16) for i in range(2)]
            s_V = [psb(f"s_V{i}", [P, D], BF16) for i in range(2)]
            s_mu = [psb(f"s_mu{i}", [P, 1], F32) for i in range(2)]
            s_sq = psb("s_sq", [P, D], F32)
            s_nmu = [psb(f"s_nmu{i}", [P, 1], F32) for i in range(2)]
            s_var = [psb(f"s_var{i}", [P, 1], F32) for i in range(2)]
            s_rstd = [psb(f"s_rstd{i}", [P, 1], F32) for i in range(2)]
            s_QTg = psb("s_QTg", [P, 8 * G], BF16)
            s_KTg = psb("s_KTg", [P, 8 * G], BF16)
            s_qsT = psb("s_qsT", [64, 8 * G], BF16)
            s_simall = psb("s_simall", [P, 512], F32)
            s_sxA = psb("s_sxA", [P, 32], F32)
            s_syA = psb("s_syA", [P, 32], F32)
            s_mA = psb("s_mA", [P, 4], F32)
            s_nmA = psb("s_nmA", [P, 4], F32)
            s_markx = psb("s_markx", [P, 4 * NK], F32)
            s_marky = psb("s_marky", [P, 4 * NK], F32)
            s_maskx = psb("s_maskx", [P, 4 * NK], F32)
            s_masky = psb("s_masky", [P, 4 * NK], F32)
            s_ex = psb("s_ex", [P, 4 * NK], F32)
            s_ey = psb("s_ey", [P, 4 * NK], F32)
            s_eexs = psb("s_eexs", [P, 4 * NK], BF16)
            s_eeys = psb("s_eeys", [P, 4 * NK], BF16)
            s_exs = psb("s_exs", [P, 32], F32)
            s_eys = psb("s_eys", [P, 32], F32)
            s_exsb = psb("s_exsb", [P, 32], BF16)
            s_eysb = psb("s_eysb", [P, 32], BF16)
            s_allE = psb("s_allE", [P, 4 * NK], BF16)
            s_top8all = psb("s_top8all", [P, 32 * NT], F32)
            s_Z = psb("s_Z", [P, 4], F32)

            for i in range(8):
                b.dma("gpsimd", s_wkT[:, D * i:D * (i + 1)], wkT.rearrange("(i p) e -> i p e", p=P)[i], "ldg")
                b.dma("gpsimd", s_wvT[:, D * i:D * (i + 1)], wvT.rearrange("(i p) e -> i p e", p=P)[i], "ldg")
                b.dma("gpsimd", s_wqT[:, D * i:D * (i + 1)], wqT.rearrange("(i p) e -> i p e", p=P)[i], "ldg")
                b.dma("gpsimd", s_wqpk[:, 512 * i:512 * (i + 1)], WqT_pk.rearrange("(i p) e -> i p e", p=P)[i], "ldg")
            b.dma("gpsimd", s_keysT[:], keysT, "ldg")
            b.dma("sync", s_bkB[:], bkB, "ld")
            b.dma("sync", s_bvB[:], bvB, "ld")
            b.dma("sync", s_bqc[:], bq_col, "ld")
            b.dma("sync", s_bkc[:], bk_col, "ld")
            for eng in ("tensor", "vector", "scalar"):
                b.wait(eng, "ldg")
                b.wait(eng, "ld")
            b.wait("gpsimd", "ldg")

            def ln_tile(buf):
                VE.reduce_sum(s_mu[buf][:], s_x[buf][:], axis=AX.X)
                VE.tensor_scalar_mul(s_nmu[buf][:], s_mu[buf][:], -1.0 / D)
                VE.tensor_scalar_mul(s_mu[buf][:], s_mu[buf][:], 1.0 / D, inc="v_mu")
                b.wait("scalar", "v_mu")
                SC.activation(s_sq[:], s_x[buf][:], func=AF.Square, bias=s_nmu[buf][:], scale=1.0,
                              accum_out=s_var[buf][:], inc="sc_var")
                b.wait("vector", "sc_var")
                VE.tensor_scalar(s_var[buf][:], s_var[buf][:], 1.0 / D, 1e-5,
                                 op0=ALU.mult, op1=ALU.add, inc="v_var")
                b.wait("scalar", "v_var")
                SC.activation(s_rstd[buf][:], s_var[buf][:], func=AF.Ln)
                SC.activation(s_rstd[buf][:], s_rstd[buf][:], func=AF.Exp, scale=-0.5, inc="sc_rstd")
                b.wait("vector", "sc_rstd")
                VE.tensor_scalar(s_xnbf[buf][:], s_x[buf][:], s_mu[buf][:], s_rstd[buf][:],
                                 op0=ALU.subtract, op1=ALU.mult, inc="v_xn")

            def transpose_tile(buf, gbuf, ti):
                b.wait("tensor", "v_xn")
                if True:
                    b.wait("tensor", "v_xnT_evac")
                for i in range(8):
                    TE.transpose(pbt[:, P * i:P * (i + 1)], s_xnbf[buf][:, P * i:P * (i + 1)], c_ident[:],
                                 inc="t_tp" if i == 7 else None)
                b.wait("vector", "t_tp")
                dst = s_xnTg[gbuf][:].rearrange("p (k t) -> p k t", k=8)[:, :, P * ti:P * (ti + 1)]
                VE.tensor_copy(dst, pbt[:].rearrange("p (k t) -> p k t", k=8), inc="v_xnT_evac")

            def kv_tile(gbuf, ti, kvbuf, do_spill, g):
                b.wait("tensor", "v_kv")
                if do_spill:
                    b.wait("vector", f"sp_kv{kvbuf}")
                for which, wT, bB, dst, scr in ((0, s_wkT, s_bkB, s_K, scr_Kn), (1, s_wvT, s_bvB, s_V, scr_V)):
                    bank = pbk[2 * which:2 * which + 2]
                    for n in range(2):
                        for k in range(8):
                            TE.matmul(bank[n][:],
                                      s_xnTg[gbuf][:, G * k + P * ti: G * k + P * (ti + 1)],
                                      wT[:, D * k + 512 * n: D * k + 512 * (n + 1)],
                                      start=(k == 0), stop=(k == 7),
                                      inc="t_kv" if (k == 7) else None)
                    b.wait("vector", "t_kv")
                    for n in range(2):
                        VE.tensor_tensor(dst[kvbuf][:, 512 * n:512 * (n + 1)], bank[n][:],
                                         bB[:, 512 * n:512 * (n + 1)], op=ALU.add,
                                         inc="v_kv" if n == 1 else None)
                    if do_spill:
                        b.wait("sync", "v_kv")
                        b.dma("sync", scr[g, P * ti:P * (ti + 1), :], dst[kvbuf][:], f"sp_kv{kvbuf}")

            for g in range(NG if max_pass >= 0 else 0):
                for ti in range(TPG):
                    t_abs = g * TPG + ti
                    buf = t_abs % 2
                    gbuf = g % 2
                    if b.cnt.get("v_xn", 0) >= 2:
                        b.wait("sync", "v_xn", b.cnt["v_xn"] - 1)
                    ldx = b.dma("sync", s_x[buf][:], x_pre.rearrange("(n p) d -> n p d", p=P)[t_abs], f"ldx{buf}")
                    b.wait("vector", f"ldx{buf}", ldx)
                    b.wait("scalar", f"ldx{buf}", ldx)
                    ln_tile(buf)
                    transpose_tile(buf, gbuf, ti)
                    b.wait("tensor", "v_xnT_evac")
                    kv_tile(gbuf, ti, buf, False, g)
                    b.wait("tensor", "v_kv")
                    for h in range(NH):
                        TE.matmul(pbk[5 + h // 8][0:64, 64 * (h % 8):64 * (h % 8) + 64],
                                  s_K[buf][:, DH * h:DH * (h + 1)],
                                  s_V[buf][:, DH * h:DH * (h + 1)],
                                  start=(t_abs == 0 and h % 8 == 0),
                                  stop=(t_abs == NT - 1 and h % 8 == 7),
                                  skip_group_check=True,
                                  inc="t_state" if h == NH - 1 else None)
            if max_pass >= 0:
                b.wait("vector", "t_state")
                VE.tensor_scalar(s_S[0:64, 0:512], pbk[5][0:64, :], c_presc[0:64, :], None,
                                 op0=ALU.mult)
                VE.tensor_scalar(s_S[0:64, 512:1024], pbk[6][0:64, :], c_presc[0:64, :], None,
                                 op0=ALU.mult, inc="v_sready")
            if (debug or zscr) and b.cnt.get("v_sready"):
                b.wait("sync", "v_sready")
                b.dma("sync", scr_S, s_S[0:64, :], "dbg")

            b.barrier("p0")

            for g in range(NG if max_pass >= 1 else 0):
                gbuf = g % 2
                b.wait("vector", f"sp_xnT{gbuf}")
                b.wait("vector", "sp_qk")
                for ti in range(TPG):
                    t_abs = g * TPG + ti
                    buf = t_abs % 2
                    if b.cnt.get("v_xn", 0) >= 2:
                        b.wait("sync", "v_xn", b.cnt["v_xn"] - 1)
                    ldx = b.dma("sync", s_x[buf][:], x_own.rearrange("(n p) d -> n p d", p=P)[t_abs], f"ldx{buf}")
                    b.wait("vector", f"ldx{buf}", ldx)
                    b.wait("scalar", f"ldx{buf}", ldx)
                    ln_tile(buf)
                    transpose_tile(buf, gbuf, ti)
                    b.wait("tensor", "v_xnT_evac")
                    kv_tile(gbuf, ti, buf, True, g)
                if p1_stage < 1:
                    continue
                b.wait("sync", "v_xnT_evac")
                b.dma("sync", scr_xnT[g].rearrange("(k p) t -> p k t", p=P),
                      s_xnTg[gbuf][:].rearrange("p (k t) -> p k t", k=8), f"sp_xnT{gbuf}")
                if p1_stage < 2:
                    continue
                for which, wT, bcol, dstT, scr in ((0, s_wqT, s_bqc, s_QTg, scr_QT), (1, s_wkT, s_bkc, s_KTg, scr_KT)):
                    for hp in range(8):
                        bank = pbk[hp % 2]
                        for k in range(8):
                            TE.matmul(bank[:],
                                      wT[:, D * k + P * hp: D * k + P * (hp + 1)],
                                      s_xnTg[gbuf][:, G * k:G * (k + 1)],
                                      start=(k == 0), stop=(k == 7),
                                      inc="t_qk" if k == 7 else None)
                        b.wait("vector", "t_qk")
                        VE.tensor_scalar(dstT[:, G * hp:G * (hp + 1)], bank[:],
                                         bcol[:, hp:hp + 1], None,
                                         op0=ALU.add, inc="v_qk")
                        b.wait("tensor", "v_qk", b.cnt["v_qk"] - 1)
                    b.wait("sync", "v_qk")
                    b.dma("sync", scr[g], dstT[:], "sp_qk")
                if p1_stage < 3:
                    continue
                b.wait("scalar", "t_sim")
                for j in range(8):
                    bank = pbk[2 + (j % 2)]
                    sub = j % 2 * 0
                    for k in range(8):
                        TE.matmul(bank[0:64, :],
                                  s_wqpk[:, 512 * k + 64 * j: 512 * k + 64 * (j + 1)],
                                  s_xnTg[gbuf][:, G * k:G * (k + 1)],
                                  start=(k == 0), stop=(k == 7),
                                  skip_group_check=True,
                                  inc="t_qs" if k == 7 else None)
                    b.wait("scalar", "t_qs")
                    SC.copy(s_qsT[0:64, G * j:G * (j + 1)], bank[0:64, :], inc="sc_qs")
                    b.wait("tensor", "sc_qs", b.cnt["sc_qs"] - 1)
                for ti in range(TPG if p1_stage >= 4 else 0):
                    t_abs = g * TPG + ti
                    b.wait("tensor", "v_simevac")
                    for p_ in range(2):
                        for h in range(4):
                            jj = p_ * 4 + h
                            TE.matmul(pbk[4][:, (p_ * 4 + h) * 64:(p_ * 4 + h) * 64 + 64],
                                      s_qsT[0:64, G * jj + P * ti:G * jj + P * (ti + 1)],
                                      s_keysT[0:64, (p_ * 4 + h) * 64:(p_ * 4 + h) * 64 + 64],
                                      start=True, stop=True, skip_group_check=True,
                                      inc="t_sim" if (p_ == 1 and h == 3) else None)
                    b.wait("vector", "t_sim")
                    b.wait("vector", "sc_exp")
                    b.wait("vector", "sp_ee")
                    VE.tensor_copy(s_simall[:], pbk[4][:], inc="v_simevac")
                    NH4 = range(4 if p1_stage >= 5 else 0)
                    rounds = 2 if t_abs == 0 else 1
                    for rnd in range(rounds):
                        if rnd == 1:
                            b.wait("vector", "sc_exp")
                        xs = lambda h: slice(h * 64, h * 64 + 64)
                        ys = lambda h: slice(256 + h * 64, 256 + h * 64 + 64)
                        for h in NH4:
                            VE.max(out=s_sxA[:, 8 * h:8 * h + 8], in_=s_simall[:, xs(h)])
                            VE.max(out=s_syA[:, 8 * h:8 * h + 8], in_=s_simall[:, ys(h)],
                                   inc="v_max" if h == 3 else None)
                        b.wait("vector", "v_max")
                        for h in NH4:
                            VE.match_replace(out=s_markx[:, 64 * h:64 * h + 64],
                                             in_to_replace=s_sxA[:, 8 * h:8 * h + 8],
                                             in_values=s_simall[:, xs(h)], imm_value=1e30)
                            VE.match_replace(out=s_marky[:, 64 * h:64 * h + 64],
                                             in_to_replace=s_syA[:, 8 * h:8 * h + 8],
                                             in_values=s_simall[:, ys(h)], imm_value=1e30,
                                             inc="v_mr" if h == 3 else None)
                        b.wait("vector", "v_mr")
                        for h in NH4:
                            VE.tensor_tensor(s_mA[:, h:h + 1], s_sxA[:, 8 * h:8 * h + 1],
                                             s_syA[:, 8 * h:8 * h + 1], op=ALU.add)
                        for h in NH4:
                            VE.tensor_scalar_mul(s_nmA[:, h:h + 1], s_mA[:, h:h + 1], -1.0,
                                                 inc="v_m" if h == 3 else None)
                        for h in NH4:
                            VE.tensor_tensor(s_maskx[:, 64 * h:64 * h + 64], s_markx[:, 64 * h:64 * h + 64],
                                             s_simall[:, xs(h)], op=ALU.not_equal)
                            VE.tensor_tensor(s_masky[:, 64 * h:64 * h + 64], s_marky[:, 64 * h:64 * h + 64],
                                             s_simall[:, ys(h)], op=ALU.not_equal,
                                             inc="v_mask" if h == 3 else None)
                        b.wait("scalar", "v_m")
                        for h in NH4:
                            SC.activation(s_ex[:, 64 * h:64 * h + 64], s_simall[:, xs(h)], func=AF.Exp,
                                          bias=s_nmA[:, h:h + 1], scale=1.0)
                            SC.activation(s_ey[:, 64 * h:64 * h + 64], s_simall[:, ys(h)], func=AF.Exp,
                                          scale=1.0)
                            SC.activation(s_exs[:, 8 * h:8 * h + 8], s_sxA[:, 8 * h:8 * h + 8], func=AF.Exp,
                                          bias=s_nmA[:, h:h + 1], scale=1.0)
                            SC.activation(s_eys[:, 8 * h:8 * h + 8], s_syA[:, 8 * h:8 * h + 8], func=AF.Exp,
                                          scale=1.0, inc="sc_exp" if h == 3 else None)
                        b.wait("vector", "sc_exp")
                        b.wait("vector", "v_mask")
                        for h in NH4:
                            VE.tensor_mul(s_eexs[:, 64 * h:64 * h + 64], s_ex[:, 64 * h:64 * h + 64],
                                          s_maskx[:, 64 * h:64 * h + 64])
                            VE.tensor_mul(s_eeys[:, 64 * h:64 * h + 64], s_ey[:, 64 * h:64 * h + 64],
                                          s_masky[:, 64 * h:64 * h + 64], inc="v_ee" if h == 3 else None)
                        VE.tensor_copy(s_exsb[:], s_exs[:])
                        VE.tensor_copy(s_eysb[:], s_eys[:])
                        for h in NH4:
                            VE.tensor_tensor(
                                s_allE[:, 64 * h:64 * h + 64].rearrange("p (a c) -> p a c", a=8),
                                s_exsb[:, 8 * h:8 * h + 8].rearrange("p (a c) -> p a c", c=1).to_broadcast([P, 8, 8]),
                                s_eysb[:, 8 * h:8 * h + 8].rearrange("p (a c) -> p a c", a=1).to_broadcast([P, 8, 8]),
                                op=ALU.mult, inc="v_alle" if h == 3 else None)
                        b.wait("vector", "v_alle")
                        for h in NH4:
                            VE.max(out=s_top8all[:, 32 * t_abs + 8 * h:32 * t_abs + 8 * h + 8],
                                   in_=s_allE[:, 64 * h:64 * h + 64],
                                   inc="v_max2" if h == 3 else None)
                    b.wait("sync", "v_ee")
                    b.dma("sync", scr_EEx[t_abs], s_eexs[:], "sp_ee")
                    b.dma("sync", scr_EEy[t_abs], s_eeys[:], "sp_ee")
            if max_pass >= 1 and p1_stage >= 5:
                for _ in range(4):
                    VE.max(out=s_wu8[:], in_=s_wu64[:])
                b.barrier("preT")
                for tp in range(NT):
                    for h in range(4):
                        VE.reduce_sum(s_Z[:, h:h + 1], s_top8all[:, 32 * tp + 8 * h:32 * tp + 8 * h + 8],
                                      axis=AX.X)
                    for h in range(4):
                        VE.tensor_copy(s_T[:, tp * 4 + h:tp * 4 + h + 1],
                                       s_top8all[:, 32 * tp + 8 * h + 7:32 * tp + 8 * h + 8])
                    for h in range(4):
                        VE.reciprocal(s_rZ[:, tp * 4 + h:tp * 4 + h + 1], s_Z[:, h:h + 1],
                                      inc="v_rz" if h == 3 else None)
            if (debug or zscr) and b.cnt.get("v_rz"):
                b.wait("sync", "v_rz")
                b.dma("sync", scr_T, s_T[:], "dbg")
                b.dma("sync", scr_rZ, s_rZ[:], "dbg")

            for snm in ("sp_xnT0", "sp_xnT1", "sp_qk", "sp_kv0", "sp_kv1", "sp_ee"):
                if b.cnt.get(snm):
                    b.wait("sync", snm)
            b.barrier("p1")

        with ExitStack() as ps:
            def psb(name, shape, dtype):
                return ps.enter_context(nc.sbuf_tensor(name, shape, dtype))

            s_WupT = psb("s_WupT", [P, 8 * I_FFN], BF16)
            s_WdnT = psb("s_WdnT", [P, 32 * D], BF16)
            s_xnTg = [psb(f"p2_xnT{i}", [P, 8 * G], BF16) for i in range(2)]
            s_sup = psb("s_sup", [P, 32 * G], BF16)
            s_dense = [psb(f"s_dense{i}", [P, D], F32) for i in range(2)]

            for i in range(8):
                b.dma("gpsimd", s_WupT[:, I_FFN * i:I_FFN * (i + 1)],
                      WupT.rearrange("(i p) e -> i p e", p=P)[i], "ldg")
            for i in range(32):
                b.dma("gpsimd", s_WdnT[:, D * i:D * (i + 1)],
                      WdownT.rearrange("(i p) e -> i p e", p=P)[i], "ldg")
            for eng in ("tensor", "vector", "scalar"):
                b.wait(eng, "ldg")

            for g in range(NG if max_pass >= 2 else 0):
                gbuf = g % 2
                if g >= 2:
                    b.wait("sync", "t_up")
                b.dma("sync", s_xnTg[gbuf][:].rearrange("p (k t) -> p k t", k=8),
                      scr_xnT[g].rearrange("(k p) t -> p k t", p=P), f"ld_xnT{gbuf}")
                b.wait("tensor", f"ld_xnT{gbuf}")
                b.wait("scalar", "t_dn")
                for m in range(32):
                    bank = pbk[m % 2]
                    if m >= 2:
                        b.wait("tensor", "sc_sup", b.cnt["sc_sup"] - 1)
                    for k in range(8):
                        TE.matmul(bank[:],
                                  s_WupT[:, I_FFN * k + P * m: I_FFN * k + P * (m + 1)],
                                  s_xnTg[gbuf][:, G * k:G * (k + 1)],
                                  start=(k == 0), stop=(k == 7),
                                  inc="t_up" if k == 7 else None)
                    b.wait("scalar", "t_up")
                    SC.activation(s_sup[:, G * m:G * (m + 1)], bank[:], func=AF.Silu, inc="sc_sup")
                b.wait("tensor", "sc_sup")
                for ti in range(TPG):
                    t_abs = g * TPG + ti
                    dbuf = ti % 2
                    banks = pbk[2 + 2 * dbuf: 4 + 2 * dbuf]
                    if b.cnt.get("v_dense", 0) >= 2:
                        b.wait("tensor", "v_dense", b.cnt["v_dense"] - 1)
                    b.wait("vector", f"sp_acc{dbuf}")
                    for m in range(32):
                        for n in range(2):
                            TE.matmul(banks[n][:],
                                      s_sup[:, G * m + P * ti: G * m + P * (ti + 1)],
                                      s_WdnT[:, D * m + 512 * n: D * m + 512 * (n + 1)],
                                      start=(m == 0), stop=(m == 31),
                                      inc="t_dn" if (m == 31 and n == 1) else None)
                    b.wait("vector", "t_dn")
                    for n in range(2):
                        VE.tensor_tensor(s_dense[dbuf][:, 512 * n:512 * (n + 1)], banks[n][:],
                                         s_woB[:, 512 * n:512 * (n + 1)], op=ALU.add,
                                         inc="v_dense" if n == 1 else None)
                    b.wait("sync", "v_dense")
                    b.dma("sync", scr_acc.rearrange("(n p) d -> n p d", p=P)[t_abs], s_dense[dbuf][:], f"sp_acc{dbuf}")

            if b.cnt.get("sp_acc0"):
                b.wait("sync", "sp_acc0")
            if b.cnt.get("sp_acc1"):
                b.wait("sync", "sp_acc1")
            b.barrier("p2")

        with ExitStack() as ps:
            def psb(name, shape, dtype):
                return ps.enter_context(nc.sbuf_tensor(name, shape, dtype))

            HE = NE // 2
            s_deT = psb("s_deT", [P, 8 * HE], BF16)
            s_ue = psb("s_ue", [P, 16 * D], BF16)
            s_xnTg = [psb(f"p3_xnT{i}", [P, 8 * G], BF16) for i in range(2)]
            s_eex = [psb(f"p3_eex{i}", [P, 4 * NK], BF16) for i in range(2)]
            s_eey = [psb(f"p3_eey{i}", [P, 4 * NK], BF16) for i in range(2)]
            s_L = [psb(f"s_L{i}", [P, HE], BF16) for i in range(2)]
            s_E = [psb(f"s_E{i}", [P, HE], BF16) for i in range(2)]
            s_m01 = psb("s_m01", [P, HE], BF16)
            s_Pm = psb("s_Pm", [P, HE], BF16)
            s_q = psb("s_q", [P, HE], BF16)
            s_Wa = [psb(f"s_Wa{i}", [P, HE], BF16) for i in range(2)]
            s_Wtmp = psb("s_Wtmp", [P, HE], BF16)
            s_Wt = psb("s_Wt", [P, HE], BF16)
            s_es = [psb(f"s_es{i}", [P, D], F32) for i in range(2)]

            stashA = [None]
            stashB = [None]

            def flush_A():
                fn = stashA[0]
                stashA[0] = None
                if fn:
                    fn()

            def flush_B():
                fn = stashB[0]
                stashB[0] = None
                if fn:
                    fn()

            def flush_stash():
                flush_A()
                flush_B()

            job = [0]
            for he in range(2 if max_pass >= 3 else 0):
                flush_stash()
                b.wait("gpsimd", "sp_es0")
                b.wait("gpsimd", "sp_es1")
                if he == 1:
                    b.wait("gpsimd", "t_es")
                for i in range(8):
                    b.dma("gpsimd", s_deT[:, HE * i:HE * (i + 1)],
                          deT.rearrange("(i p) e -> i p e", p=P)[i, :, HE * he:HE * (he + 1)], "ldg")
                for i in range(16):
                    b.dma("gpsimd", s_ue[:, D * i:D * (i + 1)],
                          ue_n.rearrange("(i p) e -> i p e", p=P)[16 * he + i], "ldg")
                for eng in ("tensor", "vector", "scalar", "gpsimd"):
                    b.wait(eng, "ldg")
                for g in range(NG):
                    gbuf = g % 2
                    b.wait("sync", "t_L")
                    b.dma("sync", s_xnTg[gbuf][:].rearrange("p (k t) -> p k t", k=8),
                          scr_xnT[g].rearrange("(k p) t -> p k t", p=P), f"ld_xnT{gbuf}")
                    b.wait("tensor", f"ld_xnT{gbuf}")
                    for ti in range(TPG):
                        t_abs = g * TPG + ti
                        j = job[0]
                        p2 = j % 2
                        if b.cnt.get("g_E", 0) >= 8:
                            b.wait("sync", "g_E", b.cnt["g_E"] - 4)
                        ldee = b.dma("sync", s_eex[p2][:], scr_EEx[t_abs], "ld_ee")
                        ldee = b.dma("sync", s_eey[p2][:], scr_EEy[t_abs], "ld_ee")
                        if b.cnt.get("v_L", 0):
                            b.wait("tensor", "v_L")
                        lbanks = (pbk[0], pbk[1], pbk[5], pbk[6])
                        for k in range(8):
                            for c in range(4):
                                TE.matmul(lbanks[c][:],
                                          s_xnTg[gbuf][:, G * k + P * ti: G * k + P * (ti + 1)],
                                          s_deT[:, HE * k + 512 * c: HE * k + 512 * (c + 1)],
                                          start=(k == 0), stop=(k == 7),
                                          inc="t_L" if (k == 7 and c == 3) else None)
                        b.wait("vector", "t_L")
                        for c in range(4):
                            VE.tensor_copy(s_L[p2][:, 512 * c:512 * (c + 1)], lbanks[c][:],
                                           inc="v_L")
                        flush_A()
                        b.wait("gpsimd", "ld_ee", ldee)
                        for h in range(4):
                            tcol = slice(t_abs * 4 + h, t_abs * 4 + h + 1)
                            eb = h % 2
                            exsl = s_eex[p2][:].rearrange("p (h n) -> p h n", h=4)[:, h, 32 * he:32 * he + 32]
                            eysl = s_eey[p2][:].rearrange("p (h n) -> p h n", h=4)[:, h]
                            if b.cnt.get("v_Pm", 0) >= 2:
                                b.wait("gpsimd", "v_Pm", b.cnt["v_Pm"] - 1)
                            GP.tensor_tensor(
                                s_E[eb][:].rearrange("p (a c) -> p a c", a=32),
                                exsl.rearrange("p (a c) -> p a c", c=1).to_broadcast([P, 32, NK]),
                                eysl.rearrange("p (a c) -> p a c", a=1).to_broadcast([P, 32, NK]),
                                op=ALU.mult, inc="g_E")
                            b.wait("vector", "g_E")
                            VE.tensor_single_scalar(s_m01[:], s_E[eb][:], s_T[:, tcol], op=ALU.is_ge)
                            VE.tensor_mul(s_Pm[:], s_m01[:], s_E[eb][:], inc="v_Pm")
                            VE.tensor_scalar_mul(s_q[:], s_Pm[:], s_rZ[:, tcol])
                            VE.tensor_mul(s_q[:], s_q[:], s_L[p2][:], inc="v_q")
                            b.wait("scalar", "v_q")
                            if h == 0:
                                SC.activation(s_Wa[p2][:], s_q[:], func=AF.Silu, inc="sc_W")
                            else:
                                SC.activation(s_Wtmp[:], s_q[:], func=AF.Silu, inc="sc_W")
                                b.wait("vector", "sc_W")
                                VE.tensor_add(s_Wa[p2][:], s_Wa[p2][:], s_Wtmp[:], inc="v_Wacc")

                        v_scW = b.cnt["sc_W"]
                        v_Wacc = b.cnt["v_Wacc"]
                        v_tes_prev = b.cnt.get("t_es", 0)

                        def mk(j=j, p2=p2, t_abs=t_abs, v_scW=v_scW, v_Wacc=v_Wacc,
                               v_tes_prev=v_tes_prev):
                            def go():
                                ep = j % 2
                                b.wait("tensor", "sc_W", v_scW)
                                b.wait("tensor", "v_Wacc", v_Wacc)
                                b.wait("scalar", "t_es", v_tes_prev)
                                for r in range(2):
                                    b.wait("tensor", "sc_Wt")
                                    for i in range(8):
                                        TE.transpose(pbt[:, P * i:P * (i + 1)],
                                                     s_Wa[p2][:, P * (8 * r + i):P * (8 * r + i + 1)],
                                                     c_ident[:],
                                                     inc="t_Wt" if i == 7 else None)
                                    b.wait("scalar", "t_Wt")
                                    SC.copy(s_Wt[:, P * 8 * r:P * 8 * (r + 1)], pbt[:], inc="sc_Wt")
                                b.wait("tensor", "sc_Wt")
                                banks = pbk[2:4]
                                if j >= 1:
                                    b.wait("tensor", "v_es", j)
                                for m in range(16):
                                    for n in range(2):
                                        TE.matmul(banks[n][:],
                                                  s_Wt[:, P * m:P * (m + 1)],
                                                  s_ue[:, D * m + 512 * n: D * m + 512 * (n + 1)],
                                                  start=(m == 0), stop=(m == 15),
                                                  inc="t_es" if (m == 15 and n == 1) else None)
                            return go

                        def mkB(j=j, t_abs=t_abs):
                            def go():
                                ep = j % 2
                                banks = pbk[2:4]
                                b.wait("vector", "t_es")
                                b.wait("vector", f"sp_es{ep}")
                                for n in range(2):
                                    VE.tensor_copy(s_es[ep][:, 512 * n:512 * (n + 1)], banks[n][:],
                                                   inc="v_es" if n == 1 else None)
                                b.wait("gpsimd", "v_es")
                                b.dma("gpsimd", scr_acc.rearrange("(n p) d -> n p d", p=P)[t_abs],
                                      s_es[ep][:], f"sp_es{ep}", accum_op=ALU.add)
                            return go

                        flush_B()
                        stashA[0] = mk()
                        stashB[0] = mkB()
                        job[0] += 1
            flush_stash()

            if b.cnt.get("sp_es0"):
                b.wait("gpsimd", "sp_es0")
            if b.cnt.get("sp_es1"):
                b.wait("gpsimd", "sp_es1")
            b.barrier("p3")

        with ExitStack() as ps:
            def psb(name, shape, dtype):
                return ps.enter_context(nc.sbuf_tensor(name, shape, dtype))

            s_woT = psb("s_woT", [P, 8 * D], BF16)
            s_QT = [psb(f"s_QT{i}", [64, NH * G], BF16) for i in range(2)]
            s_KT = [psb(f"s_KT{i}", [64, NH * G], BF16) for i in range(2)]
            s_Kn = [psb(f"s_Kn{i}", [P, 4 * D], BF16) for i in range(2)]
            s_Vn = [psb(f"s_Vn{i}", [P, 4 * D], BF16) for i in range(2)]
            s_sT = [psb(f"s_sT{i}", [P, 16 * P], BF16) for i in range(4)]
            s_AT = psb("s_AT", [P, 8 * G], BF16)
            s_xf = [psb(f"s_xf{i}", [P, D], F32) for i in range(2)]
            s_af = [psb(f"s_af{i}", [P, D], F32) for i in range(2)]
            s_out = [psb(f"s_out{i}", [P, D], F32) for i in range(2)]

            for i in range(8):
                b.dma("gpsimd", s_woT[:, D * i:D * (i + 1)], woT.rearrange("(i p) e -> i p e", p=P)[i], "ldg")
            for eng in ("tensor", "vector"):
                b.wait(eng, "ldg")
            VE.tensor_copy(s_Sbf[0:64, :], s_S[0:64, :], inc="v_sbf")
            b.wait("tensor", "v_sbf")

            for g in range(NG if max_pass >= 4 else 0):
                gp = g % 2
                if g == 0:
                    for ss in range(2):
                        b.dma("sync",
                              s_QT[gp][0:64, :].rearrange("p (hp s c) -> p hp s c", hp=8, s=2)[:, :, ss],
                              scr_QT[g][64 * ss:64 * ss + 64, :].rearrange("p (hp c) -> p hp c", hp=8),
                              f"ld_g4{gp}")
                        b.dma("sync",
                              s_KT[gp][0:64, :].rearrange("p (hp s c) -> p hp s c", hp=8, s=2)[:, :, ss],
                              scr_KT[g][64 * ss:64 * ss + 64, :].rearrange("p (hp c) -> p hp c", hp=8),
                              f"ld_g4{gp}")
                    b.dma("sync", s_Kn[gp][:].rearrange("p (c d) -> p c d", c=4),
                          scr_Kn[g].rearrange("(c p) d -> p c d", p=P), f"ld_g4{gp}")
                    b.dma("sync", s_Vn[gp][:].rearrange("p (c d) -> p c d", c=4),
                          scr_V[g].rearrange("(c p) d -> p c d", p=P), f"ld_g4{gp}")
                if g + 1 < NG:
                    b.wait("sync", "t_st")
                    for ss in range(2):
                        b.dma("sync",
                              s_QT[gp ^ 1][0:64, :].rearrange("p (hp s c) -> p hp s c", hp=8, s=2)[:, :, ss],
                              scr_QT[g + 1][64 * ss:64 * ss + 64, :].rearrange("p (hp c) -> p hp c", hp=8),
                              f"ld_g4{gp ^ 1}")
                        b.dma("sync",
                              s_KT[gp ^ 1][0:64, :].rearrange("p (hp s c) -> p hp s c", hp=8, s=2)[:, :, ss],
                              scr_KT[g + 1][64 * ss:64 * ss + 64, :].rearrange("p (hp c) -> p hp c", hp=8),
                              f"ld_g4{gp ^ 1}")
                    b.dma("sync", s_Kn[gp ^ 1][:].rearrange("p (c d) -> p c d", c=4),
                          scr_Kn[g + 1].rearrange("(c p) d -> p c d", p=P), f"ld_g4{gp ^ 1}")
                    b.dma("sync", s_Vn[gp ^ 1][:].rearrange("p (c d) -> p c d", c=4),
                          scr_V[g + 1].rearrange("(c p) d -> p c d", p=P), f"ld_g4{gp ^ 1}")
                b.wait("tensor", f"ld_g4{gp}")
                b.wait("tensor", "v_out")
                for ti in range(TPG):
                    b.wait("tensor", "v_sT")
                    for h in range(NH):
                        TE.matmul(pbk[h // 4][:, P * (h % 4):P * (h % 4 + 1)],
                                  s_KT[gp][0:64, G * h + P * ti: G * h + P * (ti + 1)],
                                  s_QT[gp][0:64, G * h + P * ti: G * h + P * (ti + 1)],
                                  start=True, stop=True, skip_group_check=True,
                                  inc="t_sc" if h == NH - 1 else None)
                    b.wait("vector", "t_sc")
                    for bk in range(4):
                        VE.tensor_tensor(
                            s_sT[ti][:].rearrange("p (h t) -> p h t", h=16)[:, 4 * bk:4 * bk + 4],
                            pbk[bk][:].rearrange("p (h t) -> p h t", h=4),
                            c_triu[:].rearrange("p (o t) -> p o t", o=1).to_broadcast([P, 4, P]),
                            op=ALU.mult, inc="v_sT" if bk == 3 else None)
                b.wait("tensor", "v_sT")
                for ti in range(TPG):
                    t_abs = g * TPG + ti
                    b.wait("tensor", "v_AT")
                    b.wait("tensor", "v_sbf")
                    for h in range(NH):
                        hp = h // 2
                        bank = pbk[hp // 4]
                        reg = bank[64 * (h % 2):64 * (h % 2) + 64, P * (hp % 4):P * (hp % 4) + P]
                        TE.matmul(reg, s_Vn[gp][:, D * ti + DH * h: D * ti + DH * (h + 1)],
                                  s_sT[ti][:, P * h:P * (h + 1)],
                                  start=True, stop=False, tile_position=(0, 64 * (h % 2)),
                                  skip_group_check=True)
                        TE.matmul(reg, s_Sbf[0:64, DH * h:DH * (h + 1)],
                                  s_QT[gp][0:64, G * h + P * ti: G * h + P * (ti + 1)],
                                  start=False, stop=True, tile_position=(0, 64 * (h % 2)),
                                  skip_group_check=True,
                                  inc="t_AT" if h == NH - 1 else None)
                    for h in range(NH):
                        TE.matmul(pbk[5 + h // 8][0:64, 64 * (h % 8):64 * (h % 8) + 64],
                                  s_Kn[gp][:, D * ti + DH * h: D * ti + DH * (h + 1)],
                                  s_Vn[gp][:, D * ti + DH * h: D * ti + DH * (h + 1)],
                                  start=True, stop=True,
                                  skip_group_check=True,
                                  inc="t_st" if h == NH - 1 else None)
                    b.wait("vector", "t_AT")
                    for bk in range(2):
                        VE.tensor_copy(
                            s_AT[:].rearrange("p (i t) -> p i t", i=8)[:, 4 * bk:4 * bk + 4, P * ti:P * (ti + 1)],
                            pbk[bk][:].rearrange("p (r t) -> p r t", r=4),
                            inc="v_AT" if bk == 1 else None)
                    b.wait("vector", "t_st")
                    VE.tensor_add(s_S[0:64, 0:512], s_S[0:64, 0:512], pbk[5][0:64, :])
                    VE.tensor_add(s_S[0:64, 512:1024], s_S[0:64, 512:1024], pbk[6][0:64, :])
                    VE.tensor_copy(s_Sbf[0:64, :], s_S[0:64, :], inc="v_sbf")
                b.wait("tensor", "v_AT")
                for ti in range(TPG):
                    t_abs = g * TPG + ti
                    obuf = ti % 2
                    banks = pbk[2 * obuf:2 * obuf + 2]
                    if b.cnt.get("v_out", 0) >= 2:
                        b.wait("sync", "v_out", b.cnt["v_out"] - 1)
                    ldf = b.dma("sync", s_xf[obuf][:], x_own.rearrange("(n p) d -> n p d", p=P)[t_abs], f"ld_fin{obuf}")
                    ldf = b.dma("sync", s_af[obuf][:], scr_acc.rearrange("(n p) d -> n p d", p=P)[t_abs], f"ld_fin{obuf}")
                    if b.cnt.get("v_out", 0) >= 2:
                        b.wait("tensor", "v_out", b.cnt["v_out"] - 1)
                    for i in range(8):
                        for n in range(2):
                            TE.matmul(banks[n][:],
                                      s_AT[:, G * i + P * ti: G * i + P * (ti + 1)],
                                      s_woT[:, D * i + 512 * n: D * i + 512 * (n + 1)],
                                      start=(i == 0), stop=(i == 7),
                                      inc="t_O" if (i == 7 and n == 1) else None)
                    b.wait("vector", "t_O")
                    b.wait("vector", f"ld_fin{obuf}", ldf)
                    b.wait("vector", f"st_out{obuf}")
                    for n in range(2):
                        VE.tensor_tensor(s_out[obuf][:, 512 * n:512 * (n + 1)], banks[n][:],
                                         s_af[obuf][:, 512 * n:512 * (n + 1)], op=ALU.add)
                    VE.tensor_add(s_out[obuf][:], s_out[obuf][:], s_xf[obuf][:], inc="v_out")
                    b.wait("sync", "v_out")
                    b.dma("sync", out_d.rearrange("(n p) d -> n p d", p=P)[t_abs], s_out[obuf][:], f"st_out{obuf}")

            if b.cnt.get("st_out0"):
                b.wait("sync", "st_out0")
                b.wait("sync", "st_out1")

        b.run(block)

    return nc


def _host_inputs(inputs):
    f32 = np.float32
    x = np.asarray(inputs["L0_attn_x"], f32)
    wq = np.asarray(inputs["L0_attn_wq_w"], f32); bq = np.asarray(inputs["L0_attn_wq_b"], f32)
    wk = np.asarray(inputs["L0_attn_wk_w"], f32); bk = np.asarray(inputs["L0_attn_wk_b"], f32)
    wv = np.asarray(inputs["L0_attn_wv_w"], f32); bv = np.asarray(inputs["L0_attn_wv_b"], f32)
    wo = np.asarray(inputs["L0_attn_wo_w"], f32); bo = np.asarray(inputs["L0_attn_wo_b"], f32)
    Wup = np.asarray(inputs["L0_ffn_W_up"], f32)
    Wdown = np.asarray(inputs["L0_ffn_W_down"], f32)
    Wq_pk = np.asarray(inputs["L0_ffn_W_q"], f32)
    keys = np.asarray(inputs["L0_ffn_keys"], f32)
    de = np.asarray(inputs["L0_ffn_down_embed"], f32)
    ue = np.asarray(inputs["L0_ffn_up_embed"], f32)

    common = {
        "wqT": np.ascontiguousarray(wq.T), "wkT": np.ascontiguousarray(wk.T),
        "wvT": np.ascontiguousarray(wv.T), "woT": np.ascontiguousarray(wo.T),
        "bq_col": np.ascontiguousarray(bq.reshape(8, P).T), "bk_col": np.ascontiguousarray(bk.reshape(8, P).T),
        "bkB": np.ascontiguousarray(np.broadcast_to(bk, (P, D))),
        "bvB": np.ascontiguousarray(np.broadcast_to(bv, (P, D))),
        "woB": np.ascontiguousarray(np.broadcast_to(bo, (P, D))),
        "WupT": np.ascontiguousarray(Wup.T), "WdownT": np.ascontiguousarray(Wdown.T),
        "WqT_pk": np.ascontiguousarray(Wq_pk.T),
        "keysT": np.ascontiguousarray(keys.transpose(3, 2, 0, 1).reshape(64, 512)),
        "deT": np.ascontiguousarray(de.T), "ue_n": np.ascontiguousarray(ue),
        "ident_bf": np.eye(P, dtype=ml_dtypes.bfloat16),
        "triu_f": np.triu(np.ones((P, P), dtype=f32)),
    }
    in_maps = []
    for c in range(8):
        bi, h = c // 2, c % 2
        m = dict(common)
        m["x_own"] = np.ascontiguousarray(x[bi, TOK * h:TOK * (h + 1)])
        m["x_pre"] = np.ascontiguousarray(x[bi, 0:TOK])
        m["pre_scale"] = np.full((P, 1), 1.0 if h == 1 else 0.0, f32)
        in_maps.append(m)
    return in_maps


def kernel(**inputs):
    in_maps = _host_inputs(inputs)
    nc = build_nc()
    res = run_bass_kernel_spmd(nc, in_maps, core_ids=list(range(8)))
    out = np.empty((4, 2 * TOK, D), np.float32)
    for c in range(8):
        bi, h = c // 2, c % 2
        out[bi, TOK * h:TOK * (h + 1)] = np.asarray(res.results[c]["out"], np.float32)
    return out
